# version 01
# baseline (speedup 1.0000x reference)
import contextlib
import numpy as np
import ml_dtypes
import concourse.bass as bass
import concourse.mybir as mybir
from concourse.bass_utils import run_bass_kernel_spmd

F32 = mybir.dt.float32
BF16 = mybir.dt.bfloat16
AF = mybir.ActivationFunctionType
ALU = mybir.AluOpType
AX = mybir.AxisListType

T = 4096
C = 1024
H = 16
DH = 64
NPAIR = 8
NROPE = 1369
PREFIX = T - NROPE
E = 8
HID = 4096
NG = 4
GT = T // NG
NTS = T // 128
QB = 512
NQB = T // QB
KT = T // 128

ENGINES = ("tensor", "vector", "scalar", "gpsimd", "sync")


class Ctr:
    __slots__ = ("name", "sem", "val")

    def __init__(self, name, sem):
        self.name = name
        self.sem = sem
        self.val = 0

    def mark(self):
        return (self, self.val)


class Prog:
    def __init__(self, nc, stack):
        self.nc = nc
        self.stack = stack
        self.stream = {e: [] for e in ENGINES}

    def ctr(self, name):
        sem = self.stack.enter_context(self.nc.semaphore(name))
        return Ctr(name, sem)

    def op(self, eng, fn, waits=(), inc=None):
        wl = []
        for w in waits:
            if w is None:
                continue
            c, v = w if isinstance(w, tuple) else (w, w.val)
            assert v <= c.val, f"wait on future value {c.name} {v} > {c.val}"
            if v > 0:
                wl.append((c, v))
        if inc is not None:
            c, n = inc
            c.val += n
        self.stream[eng].append((fn, tuple(wl), inc))

    def wait(self, eng, *waits):
        self.op(eng, None, waits=waits)

    def barrier(self, name, extra_waits=()):
        b = self.ctr(name)
        for e in ENGINES:
            w = tuple(extra_waits) if e == "sync" else ()
            self.op(e, (lambda eng: eng.drain()), waits=w, inc=(b, 1))
        for e in ENGINES:
            self.wait(e, (b, len(ENGINES)))
        return b

    def build(self, block):
        for e in ENGINES:
            self._replay(getattr(block, e), self.stream[e])

    def _replay(self, deco, ops):
        @deco
        def _(eng):
            for fn, waits, inc in ops:
                for c, v in waits:
                    eng.wait_ge(c.sem, v)
                ins = fn(eng) if fn is not None else None
                if inc is not None:
                    c, n = inc
                    if ins is None:
                        ins = eng.nop()
                    ins.then_inc(c.sem, n)


class Pool:

    def __init__(self, slots, ctrs=None):
        self.slots = list(slots)
        self.ctrs = ctrs
        self.markers = [None] * len(self.slots)
        self.i = 0

    def acquire(self):
        s = self.i % len(self.slots)
        self.i += 1
        w = self.markers[s]
        ctr = self.ctrs[s] if self.ctrs is not None else None
        return self.slots[s], ([w] if w is not None else []), s, ctr

    def release(self, s, ctr):
        self.markers[s] = ctr.mark()


def build_nc(debug=False):
    nc = bass.Bass("TRN2", target_bir_lowering=False)
    dbg_kind = "ExternalOutput" if debug else "Internal"

    x_in = nc.dram_tensor("x", [T, C], F32, kind="ExternalInput").ap()
    sin_in = nc.dram_tensor("sin", [NROPE, DH], F32, kind="ExternalInput").ap()
    cos_in = nc.dram_tensor("cos", [NROPE, DH], F32, kind="ExternalInput").ap()
    wqkv_in = nc.dram_tensor("wqkv", [C, 3 * C], F32, kind="ExternalInput").ap()
    bqkv_in = nc.dram_tensor("bqkv", [3 * C], F32, kind="ExternalInput").ap()
    wproj_in = nc.dram_tensor("wproj", [C, C], F32, kind="ExternalInput").ap()
    bproj_in = nc.dram_tensor("bproj", [C], F32, kind="ExternalInput").ap()
    wg_in = nc.dram_tensor("wg", [E, C], F32, kind="ExternalInput").ap()
    w1_in = nc.dram_tensor("w1", [E, C, HID], F32, kind="ExternalInput").ap()
    b1_in = nc.dram_tensor("b1", [E, HID], F32, kind="ExternalInput").ap()
    w2_in = nc.dram_tensor("w2", [E, HID, C], F32, kind="ExternalInput").ap()
    b2_in = nc.dram_tensor("b2", [E, C], F32, kind="ExternalInput").ap()
    idf_in = nc.dram_tensor("idf", [128, 128], F32, kind="ExternalInput").ap()
    idb_in = nc.dram_tensor("idb", [128, 128], BF16, kind="ExternalInput").ap()
    out = nc.dram_tensor("out", [T, C], F32, kind="ExternalOutput").ap()

    qt_sc = nc.dram_tensor("qt_sc", [NPAIR, 128, T], BF16, kind=dbg_kind).ap()
    kt_sc = nc.dram_tensor("kt_sc", [NPAIR, 128, T], BF16, kind=dbg_kind).ap()
    von_sc = nc.dram_tensor("von_sc", [NPAIR, 128, 2 * KT * 65], BF16, kind=dbg_kind).ap()
    ot_sc = nc.dram_tensor("ot_sc", [NPAIR, 128, T], BF16, kind=dbg_kind).ap()
    x1_sc = nc.dram_tensor("x1_sc", [T, C], F32, kind=dbg_kind).ap()
    lnx_sc = nc.dram_tensor("lnx_sc", [128, 8, T], BF16, kind=dbg_kind).ap()
    cw_sc = nc.dram_tensor("cw_sc", [128, NTS, E], F32, kind=dbg_kind).ap()
    ht_sc = nc.dram_tensor("ht_sc", [128, 16, GT], BF16, kind=dbg_kind).ap()
    y0_sc = nc.dram_tensor("y0_sc", [128, 8, C], F32, kind=dbg_kind).ap()

    ES = contextlib.ExitStack()
    with ES:
        sb = lambda *a: ES.enter_context(nc.sbuf_tensor(*a))
        psum = lambda *a: ES.enter_context(nc.psum_tensor(*a))
        p = Prog(nc, ES)

        idf = sb("idf_s", [128, 128], F32)
        idb = sb("idb_s", [128, 128], BF16)
        ones_f = sb("ones_f", [1, 128], F32)
        ones_b = sb("ones_b", [1, 128], BF16)
        maskh = sb("maskh", [128, 2, 128], BF16)
        bq_sb = sb("bq_sb", [128, 24], F32)
        bprow = sb("bprow", [1, C], BF16)
        b2row = sb("b2row", [E, C], F32)
        b1T = sb("b1T", [128, 32, E], F32)
        wgT = sb("wgT", [128, 8, E], F32)
        sinF = sb("sinF", [128, NROPE], BF16)
        cosF = sb("cosF", [128, NROPE], BF16)
        cw_all = sb("cw_all", [128, NTS, E], F32)
        cwT = sb("cwT", [8, T], F32)
        epsc = sb("epsc", [128, 1], F32)
        junk = sb("junk", [128, 1024], F32)

        c_ic = p.ctr("ic")
        c_icv = p.ctr("icv")
        c_ica = p.ctr("ica")

        SC = ES.enter_context(contextlib.ExitStack())
        sbc = lambda *a: SC.enter_context(nc.sbuf_tensor(*a))
        bq_stage = sbc("bq_stage", [24, 128], F32)
        b1_stage = sbc("b1_stage", [8, HID], F32)
        wg_stage = sbc("wg_stage", [8, C], F32)
        sin_stage = sbc("sin_stage", [128, 11, DH], F32)
        cos_stage = sbc("cos_stage", [128, 11, DH], F32)

        p.op("sync", lambda sy: sy.dma_start(out=idf[:], in_=idf_in), inc=(c_ic, 16))
        p.op("sync", lambda sy: sy.dma_start(out=idb[:], in_=idb_in), inc=(c_ic, 16))
        p.op("sync", lambda sy: sy.dma_start(out=bq_stage[:], in_=bqkv_in.rearrange("(a b) -> a b", b=128)[0:24]), inc=(c_ic, 16))
        p.op("sync", lambda sy: sy.dma_start(out=b2row[:], in_=b2_in), inc=(c_ic, 16))
        p.op("sync", lambda sy: sy.dma_start(out=b1_stage[:], in_=b1_in), inc=(c_ic, 16))
        p.op("sync", lambda sy: sy.dma_start(out=wg_stage[:], in_=wg_in), inc=(c_ic, 16))
        p.op("sync", lambda sy: sy.dma_start(out=sin_stage[0:128, 0:10, :], in_=sin_in[0:1280].rearrange("(n p) d -> p n d", p=128)), inc=(c_ic, 16))
        p.op("sync", lambda sy: sy.dma_start(out=sin_stage[0:89, 10, :], in_=sin_in[1280:NROPE, :]), inc=(c_ic, 16))
        p.op("sync", lambda sy: sy.dma_start(out=cos_stage[0:128, 0:10, :], in_=cos_in[0:1280].rearrange("(n p) d -> p n d", p=128)), inc=(c_ic, 16))
        p.op("sync", lambda sy: sy.dma_start(out=cos_stage[0:89, 10, :], in_=cos_in[1280:NROPE, :]), inc=(c_ic, 16))
        p.op("gpsimd", lambda g: g.dma_start(out=bprow[:], in_=bproj_in.rearrange("(a b) -> a b", a=1)), inc=(c_ic, 16))

        p.op("vector", lambda v: v.memset(ones_f[:], 1.0), inc=(c_icv, 1))
        p.op("vector", lambda v: v.memset(ones_b[:], 1.0), inc=(c_icv, 1))
        p.op("vector", lambda v: v.memset(maskh[:], 0.0), inc=(c_icv, 1))
        p.op("vector", lambda v: v.memset(maskh[0:64, 0, :], 1.0), waits=[c_icv.mark()], inc=(c_icv, 1))
        p.op("vector", lambda v: v.memset(maskh[64:128, 1, :], 1.0), waits=[c_icv.mark()], inc=(c_icv, 1))
        p.op("vector", lambda v: v.memset(epsc[:], 1e-5), inc=(c_icv, 1))

        pconst = SC.enter_context(nc.psum_tensor("pconst", [128, 512], F32))
        pconst2 = SC.enter_context(nc.psum_tensor("pconst2", [64, 1408], F32))
        ld_all = c_ic.mark()
        p.op("tensor", lambda t: t.transpose(pconst[:, 0:24], bq_stage[:], idf[0:24, 0:24]),
             waits=[ld_all], inc=(c_ica, 1))
        p.op("vector", lambda v: v.tensor_copy(bq_sb[:], pconst[:, 0:24]),
             waits=[c_ica.mark()], inc=(c_icv, 1))
        for hf in range(32):
            p.op("tensor", lambda t, hf=hf: t.transpose(pconst[:, 32 + hf * 8:40 + hf * 8], b1_stage[:, hf * 128:(hf + 1) * 128], idf[0:8, 0:8]),
                 waits=[ld_all, c_icv.mark()], inc=(c_ica, 1))
        p.op("vector", lambda v: v.tensor_copy(b1T[:], pconst[:, 32:32 + 256].rearrange("p (hf e) -> p hf e", e=8)),
             waits=[c_ica.mark()], inc=(c_icv, 1))
        for cc in range(8):
            p.op("tensor", lambda t, cc=cc: t.transpose(pconst[:, 320 + cc * 8:328 + cc * 8], wg_stage[:, cc * 128:(cc + 1) * 128], idf[0:8, 0:8]),
                 waits=[ld_all, c_icv.mark()], inc=(c_ica, 1))
        p.op("vector", lambda v: v.tensor_copy(wgT[:], pconst[:, 320:320 + 64].rearrange("p (cc e) -> p cc e", e=8)),
             waits=[c_ica.mark()], inc=(c_icv, 1))
        for stage, dstF, negate in ((sin_stage, sinF, True), (cos_stage, cosF, False)):
            for i in range(11):
                n = 128 if i < 10 else 89
                p.op("tensor", lambda t, i=i, n=n, stage=stage: t.transpose(
                    pconst2[:, i * 128:i * 128 + n], stage[0:n, i, :], idf[0:n, 0:n]),
                    waits=[ld_all, c_icv.mark()], inc=(c_ica, 1))
            p.op("vector", lambda v, dstF=dstF: v.tensor_copy(dstF[0:64, :], pconst2[:, 0:NROPE]),
                 waits=[c_ica.mark()], inc=(c_icv, 1))
            if negate:
                p.op("vector", lambda v, dstF=dstF: v.tensor_scalar(
                    dstF[0:32, :], dstF[0:32, :], -1.0, None, op0=ALU.mult),
                    waits=[c_icv.mark()], inc=(c_icv, 1))
            p.op("scalar", lambda s, dstF=dstF: s.dma_start(out=dstF[64:128, :], in_=dstF[0:64, :]),
                 waits=[c_icv.mark()], inc=(c_ic, 16))
        consts_done = [c_ic.mark(), c_icv.mark(), c_ica.mark()]
        SC.close()

        S01 = ES.enter_context(contextlib.ExitStack())
        sb1 = lambda *a: S01.enter_context(nc.sbuf_tensor(*a))
        ps1 = lambda *a: S01.enter_context(nc.psum_tensor(*a))
        xnT = sb1("xnT", [128, 8, T], BF16)
        wqkv_bf = sb1("wqkv_bf", [128, 8, 3 * C], BF16)
        xt = [sb1(f"xt{i}", [128, C], F32) for i in range(2)]
        xnb = [sb1(f"xn{i}", [128, C], BF16) for i in range(2)]
        stat = sb1("stat", [128, 8], F32)
        rstd2 = sb1("rstd2", [128, 2], F32)
        ptr = [ps1("ptr0", [128, 128], BF16), ps1("ptr1", [128, 128], BF16)]

        c_x = [p.ctr("x_ld0"), p.ctr("x_ld1")]
        c_wq = p.ctr("wq_ld")
        c_lnv = p.ctr("lnv")
        c_lna = p.ctr("lna")
        c_tr = p.ctr("tr")
        c_xnt = p.ctr("xnt")

        wv = wqkv_in.rearrange("(cc p) f -> cc p f", p=128)
        for cc in range(8):
            p.op("gpsimd", lambda g, cc=cc: g.dma_start(out=wqkv_bf[:, cc, :], in_=wv[cc]),
                 waits=(consts_done if cc == 0 else ()), inc=(c_wq, 16))
        wq_done = c_wq.mark()

        xview = x_in.rearrange("(n p) c -> n p c", p=128)
        xtile_free = [None, None]
        xn_free = [None, None]
        for ts in range(NTS):
            b = ts % 2
            w = [consts_done[0], consts_done[1], consts_done[2]] if ts < 2 else []
            if xtile_free[b] is not None:
                w.append(xtile_free[b])
            p.op("sync", lambda sy, ts=ts, b=b: sy.dma_start(out=xt[b][:], in_=xview[ts]),
                 waits=w, inc=(c_x[b], 16))
            xld = c_x[b].mark()
            o = 4 * b
            p.op("scalar", lambda s, b=b, o=o: s.activation(junk[:], xt[b][:], AF.Identity, bias=0.0, scale=1.0, accum_out=stat[:, o:o + 1]),
                 waits=[xld, c_lnv.mark()], inc=(c_lna, 1))
            p.op("scalar", lambda s, b=b, o=o: s.activation(junk[:], xt[b][:], AF.Square, bias=0.0, scale=1.0, accum_out=stat[:, o + 1:o + 2]),
                 waits=[c_lna.mark()], inc=(c_lna, 1))
            acc_done = c_lna.mark()
            p.op("vector", lambda v, o=o: v.tensor_scalar(stat[:, o + 2:o + 3], stat[:, o:o + 1], 1.0 / C, None, op0=ALU.mult),
                 waits=[acc_done], inc=(c_lnv, 1))
            p.op("vector", lambda v, o=o: v.tensor_tensor(stat[:, o + 3:o + 4], stat[:, o + 2:o + 3], stat[:, o + 2:o + 3], op=ALU.mult),
                 waits=[c_lnv.mark()], inc=(c_lnv, 1))
            p.op("vector", lambda v, o=o: v.scalar_tensor_tensor(stat[:, o + 3:o + 4], stat[:, o + 1:o + 2], 1.0 / C, stat[:, o + 3:o + 4], op0=ALU.mult, op1=ALU.subtract),
                 waits=[c_lnv.mark()], inc=(c_lnv, 1))
            p.op("scalar", lambda s, o=o: s.activation(stat[:, o + 3:o + 4], stat[:, o + 3:o + 4], AF.Ln, bias=epsc[:], scale=1.0),
                 waits=[c_lnv.mark()], inc=(c_lna, 1))
            p.op("scalar", lambda s, b=b, o=o: s.activation(rstd2[:, b:b + 1], stat[:, o + 3:o + 4], AF.Exp, bias=0.0, scale=-0.5),
                 waits=[c_lna.mark()], inc=(c_lna, 1))
            w2_ = [c_lna.mark()]
            if xn_free[b] is not None:
                w2_.append(xn_free[b])
            p.op("vector", lambda v, b=b, o=o: v.tensor_scalar(xnb[b][:], xt[b][:], stat[:, o + 2:o + 3], rstd2[:, b:b + 1], op0=ALU.subtract, op1=ALU.mult),
                 waits=w2_, inc=(c_lnv, 1))
            xtile_free[b] = c_lnv.mark()
            xn_done = c_lnv.mark()
            for cc in range(8):
                pt = ptr[cc % 2]
                tw = [xn_done]
                gidx = ts * 8 + cc
                if gidx >= 2:
                    tw.append((c_xnt, gidx - 1))
                p.op("tensor", lambda t, b=b, cc=cc, pt=pt: t.transpose(pt[:], xnb[b][:, cc * 128:(cc + 1) * 128], idb[:]),
                     waits=tw, inc=(c_tr, 1))
                p.op("vector", lambda v, ts=ts, cc=cc, pt=pt: v.tensor_copy(xnT[:, cc, ts * 128:(ts + 1) * 128], pt[:]),
                     waits=[c_tr.mark()], inc=(c_xnt, 1))
            xn_free[b] = c_xnt.mark()
        xnT_done = c_xnt.mark()

        qt_buf = sb1("qt_buf", [128, T], BF16)
        kt_buf = sb1("kt_buf", [128, T], BF16)
        vt_buf = sb1("vt_buf", [128, T], BF16)
        von_buf = sb1("von_buf", [128, 2, KT, 65], BF16)
        rot = sb1("rot", [128, NROPE], BF16)
        rtmp = sb1("rtmp", [128, NROPE], BF16)
        pqk = [ps1(f"pqk{i}", [128, 512], F32) for i in range(4)]
        pvt = [ps1(f"pvt{i}", [128, 128], BF16) for i in range(2)]

        c_p1mm = p.ctr("p1mm")
        c_p1ev = p.ctr("p1ev")
        c_rope = p.ctr("rope")
        c_spq = p.ctr("spq")
        c_spk = p.ctr("spk")
        c_spv = p.ctr("spv")
        c_vcp = p.ctr("vcp")

        qk_pool = Pool(pqk)
        vt_pool = Pool(pvt)

        def proj_chunk(fc, dst_buf, buf_free):
            for tb in range(8):
                pb, pw, ps_, _ = qk_pool.acquire()
                for cbk in range(8):
                    w = []
                    if cbk == 0:
                        w += pw
                        if tb == 0:
                            w += [xnT_done, wq_done]
                            if buf_free is not None:
                                w.append(buf_free)
                    p.op("tensor", lambda t, fc=fc, tb=tb, cbk=cbk, pb=pb: t.matmul(
                        pb[:], wqkv_bf[:, cbk, fc * 128:(fc + 1) * 128],
                        xnT[:, cbk, tb * 512:(tb + 1) * 512],
                        start=(cbk == 0), stop=(cbk == 7)),
                        waits=w, inc=(c_p1mm, 1) if cbk == 7 else None)
                p.op("scalar", lambda s, fc=fc, tb=tb, pb=pb, dst_buf=dst_buf: s.activation(
                    dst_buf[:, tb * 512:(tb + 1) * 512], pb[:], AF.Identity,
                    bias=bq_sb[:, fc:fc + 1], scale=1.0),
                    waits=[c_p1mm.mark()], inc=(c_p1ev, 1))
                qk_pool.release(ps_, c_p1ev)
            return c_p1ev.mark()

        def rope_inplace(dst_buf, ev_done):
            for (a, b_) in ((0, 32), (32, 0), (64, 96), (96, 64)):
                p.op("scalar", lambda s, a=a, b_=b_, dst_buf=dst_buf: s.dma_start(
                    out=rot[a:a + 32, :], in_=dst_buf[b_:b_ + 32, PREFIX:T]),
                    waits=[ev_done, c_rope.mark()], inc=(c_rope, 16))
            p.op("vector", lambda v: v.tensor_tensor(rtmp[:], rot[:], sinF[:], op=ALU.mult),
                 waits=[c_rope.mark()], inc=(c_rope, 1))
            p.op("vector", lambda v, dst_buf=dst_buf: v.tensor_tensor(rot[:], dst_buf[:, PREFIX:T], cosF[:], op=ALU.mult),
                 waits=[c_rope.mark()], inc=(c_rope, 1))
            p.op("vector", lambda v, dst_buf=dst_buf: v.tensor_tensor(dst_buf[:, PREFIX:T], rot[:], rtmp[:], op=ALU.add),
                 waits=[c_rope.mark()], inc=(c_rope, 1))
            return c_rope.mark()

        pq_free = pk_free = pv_free = pvon_free = None
        for pair in range(NPAIR):
            ev = proj_chunk(pair, qt_buf, pq_free)
            rp = rope_inplace(qt_buf, ev)
            p.op("scalar", lambda s, pair=pair: s.dma_start(out=qt_sc[pair], in_=qt_buf[:]),
                 waits=[rp], inc=(c_spq, 16))
            pq_free = c_spq.mark()
            ev = proj_chunk(8 + pair, kt_buf, pk_free)
            rp = rope_inplace(kt_buf, ev)
            p.op("scalar", lambda s, pair=pair: s.dma_start(out=kt_sc[pair], in_=kt_buf[:]),
                 waits=[rp], inc=(c_spk, 16))
            pk_free = c_spk.mark()
            ev = proj_chunk(16 + pair, vt_buf, pv_free)
            w = [pvon_free] if pvon_free is not None else []
            p.op("vector", lambda v: v.memset(von_buf[:, :, :, 64:65], 1.0),
                 waits=w + [c_vcp.mark()], inc=(c_vcp, 1))
            for kt in range(KT):
                pt, pw, ps_, _ = vt_pool.acquire()
                p.op("tensor", lambda t, kt=kt, pt=pt: t.transpose(
                    pt[:], vt_buf[:, kt * 128:(kt + 1) * 128], idb[:]),
                    waits=pw + ([ev] if kt == 0 else []), inc=(c_tr, 1))
                trm = c_tr.mark()
                p.op("vector", lambda v, kt=kt, pt=pt: v.tensor_copy(von_buf[:, 0, kt, 0:64], pt[:, 0:64]),
                     waits=[trm], inc=(c_vcp, 1))
                p.op("vector", lambda v, kt=kt, pt=pt: v.tensor_copy(von_buf[:, 1, kt, 0:64], pt[:, 64:128]),
                     waits=[trm], inc=(c_vcp, 1))
                vt_pool.release(ps_, c_vcp)
            pv_free = c_tr.mark()
            p.op("scalar", lambda s, pair=pair: s.dma_start(
                out=von_sc[pair], in_=von_buf[:].rearrange("p h kt f -> p (h kt f)")),
                waits=[c_vcp.mark()], inc=(c_spv, 16))
            pvon_free = c_spv.mark()

        p.barrier("end_p1", extra_waits=[c_spq.mark(), c_spk.mark(), c_spv.mark(),
                                         c_x[0].mark(), c_x[1].mark(), c_rope.mark(),
                                         c_ic.mark(), c_wq.mark()])
        S01.close()

        S2 = ES.enter_context(contextlib.ExitStack())
        sb2 = lambda *a: S2.enter_context(nc.sbuf_tensor(*a))
        ps2 = lambda *a: S2.enter_context(nc.psum_tensor(*a))
        qs_ = [sb2(f"qs{i}", [128, T], BF16) for i in range(2)]
        ks_ = [sb2(f"ks{i}", [128, T], BF16) for i in range(2)]
        vo_ = [sb2(f"vo{i}", [128, 2, KT, 65], BF16) for i in range(2)]
        strips = [sb2(f"strip{i}", [128, KT, QB], BF16) for i in range(2)]
        otu_ = [sb2(f"otu{i}", [65, QB], F32) for i in range(2)]
        dq_ = [sb2(f"dq{i}", [1, QB], F32) for i in range(2)]
        rq_ = [sb2(f"rq{i}", [1, QB], F32) for i in range(2)]
        otp_ = [sb2(f"otp{i}", [128, T], BF16) for i in range(2)]
        sqb = sb2("sqb", [128, T], BF16)
        nbuf = sb2("nbuf", [128, 16], F32)
        nmc = sb2("nmc", [128, 4], F32)
        s2p = [ps2(f"s2p{i}", [128, 2, 512], F32) for i in range(3)]
        potp = [ps2(f"potp{i}", [65, QB], F32) for i in range(2)]

        c_alq = [p.ctr("alq0"), p.ctr("alq1")]
        c_alk = [p.ctr("alk0"), p.ctr("alk1")]
        c_alv = [p.ctr("alv0"), p.ctr("alv1")]
        c_smm = p.ctr("smm")
        c_ex = p.ctr("ex")
        c_pv = p.ctr("pv")
        c_oev = p.ctr("oev")
        c_dqm = [p.ctr("dqm0"), p.ctr("dqm1")]
        c_rq = p.ctr("rqm")
        c_bc = p.ctr("bc")
        c_fin = p.ctr("fin")
        c_nrm = p.ctr("nrm")
        c_osp = [p.ctr("osp0"), p.ctr("osp1")]

        qpool = Pool(qs_, c_alq)
        kpool = Pool(ks_, c_alk)
        vpool = Pool(vo_, c_alv)
        spool = Pool(strips)
        opool = Pool(otu_)
        dqpool = Pool(dq_, c_dqm)
        rqpool = Pool(rq_)
        otpool = Pool(otp_)
        s2pool = Pool(s2p)
        potpool = Pool(potp)

        def load_pair(pair):
            qsl, qw, qi, qc = qpool.acquire()
            ksl, kw, ki, kc = kpool.acquire()
            vol, vw, vi, vc = vpool.acquire()
            p.op("sync", lambda sy, pair=pair, qsl=qsl: sy.dma_start(out=qsl[:], in_=qt_sc[pair]),
                 waits=qw, inc=(qc, 16))
            p.op("sync", lambda sy, pair=pair, ksl=ksl: sy.dma_start(out=ksl[:], in_=kt_sc[pair]),
                 waits=kw, inc=(kc, 16))
            p.op("sync", lambda sy, pair=pair, vol=vol: sy.dma_start(
                out=vol[:].rearrange("p h kt f -> p (h kt f)"), in_=von_sc[pair]),
                 waits=vw, inc=(vc, 16))
            return (qsl, qi), (ksl, ki), (vol, vi), [qc.mark(), kc.mark(), vc.mark()]

        pair_ex_end = []
        pend = load_pair(0)
        for pair in range(NPAIR):
            (qsl, qi), (ksl, ki), (vol, vi), ldm = pend
            if pair + 1 < NPAIR:
                pend = load_pair(pair + 1)
            nmo = (pair % 2) * 2
            for side, buf in ((0, qsl), (1, ksl)):
                p.op("scalar", lambda s, buf=buf: s.activation(sqb[:], buf[:], AF.Square, bias=0.0, scale=1.0),
                     waits=ldm + [c_nrm.mark()], inc=(c_nrm, 1))
                sq_done = c_nrm.mark()
                for h in range(2):
                    ni = side * 2 + h
                    s2l, s2w, s2i, _ = s2pool.acquire()
                    for blk in range(8):
                        p.op("tensor", lambda t, h=h, blk=blk, s2l=s2l: t.matmul(
                            s2l[:, blk % 2, :], maskh[:, h, :], sqb[:, blk * 512:(blk + 1) * 512],
                            start=True, stop=True),
                            waits=[sq_done, c_nrm.mark()] + (s2w if blk == 0 else []), inc=(c_nrm, 1))
                        p.op("vector", lambda v, ni=ni, blk=blk, s2l=s2l: v.reduce_max(
                            nbuf[:, 8 + ni:9 + ni], s2l[:, blk % 2, :], axis=AX.X),
                            waits=[c_nrm.mark()], inc=(c_nrm, 1))
                        if blk == 0:
                            p.op("vector", lambda v, ni=ni: v.tensor_copy(nbuf[:, ni:ni + 1], nbuf[:, 8 + ni:9 + ni]),
                                 waits=[c_nrm.mark()], inc=(c_nrm, 1))
                        else:
                            p.op("vector", lambda v, ni=ni: v.tensor_tensor(
                                nbuf[:, ni:ni + 1], nbuf[:, ni:ni + 1], nbuf[:, 8 + ni:9 + ni], op=ALU.max),
                                waits=[c_nrm.mark()], inc=(c_nrm, 1))
                    s2pool.release(s2i, c_nrm)
            nmc_war = (c_ex, pair_ex_end[pair - 2]) if pair >= 2 else None
            for h in range(2):
                p.op("vector", lambda v, h=h: v.tensor_tensor(nbuf[:, 12 + h:13 + h], nbuf[:, h:h + 1], nbuf[:, 2 + h:3 + h], op=ALU.mult),
                     waits=[c_nrm.mark()], inc=(c_nrm, 1))
                p.op("scalar", lambda s, h=h: s.activation(nbuf[:, 12 + h:13 + h], nbuf[:, 12 + h:13 + h], AF.Ln, bias=epsc[:], scale=1.0),
                     waits=[c_nrm.mark()], inc=(c_nrm, 1))
                p.op("scalar", lambda s, h=h: s.activation(nbuf[:, 12 + h:13 + h], nbuf[:, 12 + h:13 + h], AF.Exp, bias=0.0, scale=0.5),
                     waits=[c_nrm.mark()], inc=(c_nrm, 1))
                p.op("vector", lambda v, h=h, nmo=nmo: v.tensor_scalar(nmc[:, nmo + h:nmo + h + 1], nbuf[:, 12 + h:13 + h], -0.127575, None, op0=ALU.mult),
                     waits=[c_nrm.mark(), nmc_war], inc=(c_nrm, 1))
            nm_done = c_nrm.mark()

            otsl, otw, oti, _ = otpool.acquire()
            for h in range(2):
                for qb in range(NQB):
                    stl, stw, sti, _ = spool.acquire()
                    for ktg in range(KT // 2):
                        s2l, s2w, s2i, _ = s2pool.acquire()
                        for sub in range(2):
                            kt = 2 * ktg + sub
                            w = list(s2w) if sub == 0 else []
                            if ktg == 0 and sub == 0:
                                w += [nm_done] + stw
                            p.op("tensor", lambda t, h=h, qb=qb, kt=kt, sub=sub, s2l=s2l, ksl=ksl, qsl=qsl: t.matmul(
                                s2l[:, sub, :], ksl[h * 64:(h + 1) * 64, kt * 128:(kt + 1) * 128],
                                qsl[h * 64:(h + 1) * 64, qb * QB:(qb + 1) * QB],
                                start=True, stop=True),
                                waits=w, inc=(c_smm, 1) if sub == 1 else None)
                        p.op("scalar", lambda s, h=h, ktg=ktg, s2l=s2l, stl=stl, nmo=nmo: s.activation(
                            stl[:, 2 * ktg:2 * ktg + 2, :], s2l[:], AF.Exp,
                            bias=nmc[:, nmo + h:nmo + h + 1], scale=0.125),
                            waits=[c_smm.mark()], inc=(c_ex, 1))
                        s2pool.release(s2i, c_ex)
                    ex_done = c_ex.mark()
                    pol, pow_, poi, _ = potpool.acquire()
                    for kt in range(KT):
                        w = []
                        if kt == 0:
                            w = [ex_done] + pow_
                        p.op("tensor", lambda t, h=h, kt=kt, pol=pol, vol=vol, stl=stl: t.matmul(
                            pol[:], vol[:, h, kt, :], stl[:, kt, :],
                            start=(kt == 0), stop=(kt == KT - 1)),
                            waits=w, inc=(c_pv, 1) if kt == KT - 1 else None)
                    spool.release(sti, c_pv)
                    pv_done = c_pv.mark()
                    oul, ouw, oui, _ = opool.acquire()
                    p.op("vector", lambda v, oul=oul, pol=pol: v.tensor_copy(oul[:], pol[:]),
                         waits=[pv_done] + ouw, inc=(c_oev, 1))
                    ev_done = c_oev.mark()
                    dql, dqw, dqi, dqc = dqpool.acquire()
                    p.op("scalar", lambda s, dql=dql, oul=oul: s.dma_start(out=dql[:], in_=oul[64:65, :]),
                         waits=[ev_done] + dqw, inc=(dqc, 16))
                    rql, rqw, rqi, _ = rqpool.acquire()
                    p.op("vector", lambda v, rql=rql, dql=dql: v.reciprocal(rql[:], dql[:]),
                         waits=[dqc.mark()] + rqw, inc=(c_rq, 1))
                    dqpool.release(dqi, c_rq)
                    p.op("tensor", lambda t, pol=pol, rql=rql: t.matmul(
                        pol[0:64, :], ones_f[0:1, 0:64], rql[:], start=True, stop=True),
                        waits=[c_rq.mark(), ev_done], inc=(c_bc, 1))
                    rqpool.release(rqi, c_bc)
                    p.op("vector", lambda v, h=h, qb=qb, otsl=otsl, oul=oul, pol=pol: v.tensor_tensor(
                        otsl[h * 64:(h + 1) * 64, qb * QB:(qb + 1) * QB],
                        oul[0:64, :], pol[0:64, :], op=ALU.mult),
                        waits=[c_bc.mark()] + (otw if (h == 0 and qb == 0) else []), inc=(c_fin, 1))
                    potpool.release(poi, c_fin)
                    opool.release(oui, c_fin)
            qpool.release(qi, c_pv)
            kpool.release(ki, c_pv)
            vpool.release(vi, c_pv)
            oc = c_osp[pair % 2]
            p.op("sync", lambda sy, pair=pair, otsl=otsl: sy.dma_start(out=ot_sc[pair], in_=otsl[:]),
                 waits=[c_fin.mark()], inc=(oc, 16))
            otpool.release(oti, oc)
            pair_ex_end.append(c_ex.val)

        p.barrier("end_p2", extra_waits=[c_osp[0].mark(), c_osp[1].mark(),
                                         c_alq[0].mark(), c_alq[1].mark(),
                                         c_alk[0].mark(), c_alk[1].mark(),
                                         c_alv[0].mark(), c_alv[1].mark(),
                                         c_dqm[0].mark(), c_dqm[1].mark()])
        S2.close()

        S3 = ES.enter_context(contextlib.ExitStack())
        sb3 = lambda *a: S3.enter_context(nc.sbuf_tensor(*a))
        ps3 = lambda *a: S3.enter_context(nc.psum_tensor(*a))
        wproj_bf = sb3("wproj_bf", [128, 8, C], BF16)
        otc_ = [sb3(f"otc{i}", [128, 8, 128], BF16) for i in range(2)]
        xti_ = [sb3(f"xti{i}", [128, C], F32) for i in range(2)]
        x1t_ = [sb3(f"x1t{i}", [128, C], F32) for i in range(2)]
        xTf_ = [sb3(f"xTf{i}", [128, C], F32) for i in range(2)]
        ln1b_ = [sb3(f"ln1b{i}", [128, 8, 128], BF16) for i in range(2)]
        xn1_ = [sb3(f"xn1_{i}", [128, C], F32) for i in range(2)]
        gbuf = sb3("gbuf", [128, 10], F32)
        gb2 = sb3("gb2", [128, 10], F32)
        gwk = sb3("gwk", [128, 8], F32)
        gwk2 = sb3("gwk2", [128, 8], F32)
        pproj = [ps3(f"pproj{i}", [128, C], F32) for i in range(2)]
        ptr3 = ps3("ptr3", [128, C], F32)
        plog = ps3("plog", [128, 8], F32)
        pcwt = ps3("pcwt", [8, 128], F32)

        c_wpl = p.ctr("wpl")
        c_otl = [p.ctr("otl0"), p.ctr("otl1")]
        c_xtl = [p.ctr("xtl0"), p.ctr("xtl1")]
        c_sx1 = [p.ctr("sx10"), p.ctr("sx11")]
        c_sln = [p.ctr("sln0"), p.ctr("sln1")]
        c_scw = p.ctr("scw")
        c_p3mm = p.ctr("p3mm")
        c_p3v = p.ctr("p3v")
        c_p3a = p.ctr("p3a")
        c_p3tr = p.ctr("p3tr")

        wpv = wproj_in.rearrange("(cc p) f -> cc p f", p=128)
        for cc in range(8):
            p.op("gpsimd", lambda g, cc=cc: g.dma_start(out=wproj_bf[:, cc, :], in_=wpv[cc]),
                 inc=(c_wpl, 16))
        wp_done = c_wpl.mark()

        otcpool = Pool(otc_, c_otl)
        xtipool = Pool(xti_, c_xtl)
        x1pool = Pool(x1t_)
        xTfpool = Pool(xTf_)
        ln1pool = Pool(ln1b_)
        xn1pool = Pool(xn1_)
        projpool = Pool(pproj)
        x1store = [None, None]
        lnstore = [None, None]
        outv = out.rearrange("(n p) c -> n p c", p=128)
        x1v = x1_sc.rearrange("(n p) c -> n p c", p=128)
        ptr3_free = None
        plog_free = None
        pcwt_free = None

        for ts in range(NTS):
            otcl, ow, oi, occ = otcpool.acquire()
            for pc in range(8):
                p.op("sync", lambda sy, ts=ts, pc=pc, otcl=otcl: sy.dma_start(
                    out=otcl[:, pc, :], in_=ot_sc[pc][:, ts * 128:(ts + 1) * 128]),
                    waits=(ow if pc == 0 else []), inc=(occ, 16))
            otm = occ.mark()
            xtil, xw, xi, xcc = xtipool.acquire()
            p.op("sync", lambda sy, ts=ts, xtil=xtil: sy.dma_start(out=xtil[:], in_=xview[ts]),
                 waits=xw, inc=(xcc, 16))
            xtm = xcc.mark()
            ppl, ppw, ppi, _ = projpool.acquire()
            for half in range(2):
                for pc in range(8):
                    w = []
                    if pc == 0 and half == 0:
                        w = [otm, wp_done] + ppw
                    p.op("tensor", lambda t, pc=pc, half=half, ppl=ppl, otcl=otcl: t.matmul(
                        ppl[:, half * 512:(half + 1) * 512], otcl[:, pc, :],
                        wproj_bf[:, pc, half * 512:(half + 1) * 512],
                        start=(pc == 0), stop=False),
                        waits=w)
                p.op("tensor", lambda t, half=half, ppl=ppl: t.matmul(
                    ppl[:, half * 512:(half + 1) * 512], ones_b[0:1, :],
                    bprow[0:1, half * 512:(half + 1) * 512],
                    start=False, stop=True),
                    inc=(c_p3mm, 1))
            mmd = c_p3mm.mark()
            otcpool.release(oi, c_p3mm)
            x1l, x1w, x1i, _ = x1pool.acquire()
            p.op("vector", lambda v, x1l=x1l, ppl=ppl, xtil=xtil: v.scalar_tensor_tensor(
                x1l[:], ppl[:], 1.0, xtil[:], op0=ALU.mult, op1=ALU.add),
                waits=[mmd, xtm] + x1w, inc=(c_p3v, 1))
            xtipool.release(xi, c_p3v)
            projpool.release(ppi, c_p3v)
            x1_done = c_p3v.mark()
            p.op("scalar", lambda s, x1l=x1l: s.activation(junk[:], x1l[:], AF.Identity, bias=0.0, scale=1.0, accum_out=gbuf[:, 8:9]),
                 waits=[x1_done, c_p3v.mark()], inc=(c_p3a, 1))
            p.op("scalar", lambda s, x1l=x1l: s.activation(junk[:], x1l[:], AF.Square, bias=0.0, scale=1.0, accum_out=gbuf[:, 9:10]),
                 waits=[c_p3a.mark()], inc=(c_p3a, 1))
            p.op("vector", lambda v: v.tensor_scalar(gb2[:, 0:1], gbuf[:, 8:9], 1.0 / C, None, op0=ALU.mult),
                 waits=[c_p3a.mark()], inc=(c_p3v, 1))
            p.op("vector", lambda v: v.tensor_tensor(gb2[:, 1:2], gb2[:, 0:1], gb2[:, 0:1], op=ALU.mult),
                 waits=[c_p3v.mark()], inc=(c_p3v, 1))
            p.op("vector", lambda v: v.scalar_tensor_tensor(gb2[:, 1:2], gbuf[:, 9:10], 1.0 / C, gb2[:, 1:2], op0=ALU.mult, op1=ALU.subtract),
                 waits=[c_p3v.mark()], inc=(c_p3v, 1))
            p.op("scalar", lambda s: s.activation(gb2[:, 1:2], gb2[:, 1:2], AF.Ln, bias=epsc[:], scale=1.0),
                 waits=[c_p3v.mark()], inc=(c_p3a, 1))
            p.op("scalar", lambda s: s.activation(gb2[:, 2:3], gb2[:, 1:2], AF.Exp, bias=0.0, scale=-0.5),
                 waits=[c_p3a.mark()], inc=(c_p3a, 1))
            xn1l, xnw, xni, _ = xn1pool.acquire()
            p.op("vector", lambda v, x1l=x1l, xn1l=xn1l: v.tensor_scalar(
                xn1l[:], x1l[:], gb2[:, 0:1], gb2[:, 2:3], op0=ALU.subtract, op1=ALU.mult),
                waits=[c_p3a.mark()] + xnw, inc=(c_p3v, 1))
            xn1_done = c_p3v.mark()
            sxc = c_sx1[ts % 2]
            p.op("sync", lambda sy, ts=ts, x1l=x1l: sy.dma_start(out=x1v[ts], in_=x1l[:]),
                 waits=[xn1_done], inc=(sxc, 16))
            x1pool.release(x1i, sxc)
            for cc in range(8):
                w = [xn1_done]
                if cc == 0 and ptr3_free is not None:
                    w.append(ptr3_free)
                p.op("tensor", lambda t, cc=cc, xn1l=xn1l: t.transpose(
                    ptr3[:, cc * 128:(cc + 1) * 128], xn1l[:, cc * 128:(cc + 1) * 128], idf[:]),
                    waits=w, inc=(c_p3tr, 1))
            trd = c_p3tr.mark()
            xn1pool.release(xni, c_p3tr)
            xTfl, xfw, xfi, _ = xTfpool.acquire()
            p.op("vector", lambda v, xTfl=xTfl: v.tensor_copy(xTfl[:], ptr3[:]),
                 waits=[trd] + xfw, inc=(c_p3v, 1))
            ln1l, lnw, lni, _ = ln1pool.acquire()
            p.op("vector", lambda v, ln1l=ln1l: v.tensor_copy(
                ln1l[:], ptr3[:].rearrange("p (cc t) -> p cc t", t=128)),
                waits=[trd] + lnw, inc=(c_p3v, 1))
            ptr3_free = c_p3v.mark()
            slc = c_sln[ts % 2]
            p.op("scalar", lambda s, ts=ts, ln1l=ln1l: s.dma_start(
                out=lnx_sc[:, :, ts * 128:(ts + 1) * 128], in_=ln1l[:]),
                waits=[c_p3v.mark()], inc=(slc, 16))
            ln1pool.release(lni, slc)
            for cc in range(8):
                w = [c_p3v.mark()]
                if cc == 0 and plog_free is not None:
                    w.append(plog_free)
                p.op("tensor", lambda t, cc=cc, xTfl=xTfl: t.matmul(
                    plog[:], xTfl[:, cc * 128:(cc + 1) * 128], wgT[:, cc, :],
                    start=(cc == 0), stop=(cc == 7)),
                    waits=w, inc=(c_p3mm, 1) if cc == 7 else None)
            xTfpool.release(xfi, c_p3mm)
            gmm = c_p3mm.mark()
            p.op("vector", lambda v: v.reduce_max(gbuf[:, 8:9], plog[:], axis=AX.X, negate=True),
                 waits=[gmm, c_p3v.mark()], inc=(c_p3v, 1))
            p.op("scalar", lambda s: s.activation(gbuf[:, 0:8], plog[:], AF.Exp, bias=gbuf[:, 8:9], scale=1.0, accum_out=gbuf[:, 9:10]),
                 waits=[c_p3v.mark()], inc=(c_p3a, 1))
            plog_free = c_p3a.mark()
            p.op("vector", lambda v: v.reciprocal(gb2[:, 3:4], gbuf[:, 9:10]),
                 waits=[c_p3a.mark()], inc=(c_p3v, 1))
            p.op("vector", lambda v: v.reduce_max(gb2[:, 4:5], gbuf[:, 0:8], axis=AX.X),
                 waits=[c_p3a.mark()], inc=(c_p3v, 1))
            p.op("vector", lambda v: v.tensor_scalar(gwk[:], gbuf[:, 0:8], gb2[:, 4:5], None, op0=ALU.is_ge),
                 waits=[c_p3v.mark()], inc=(c_p3v, 1))
            p.op("vector", lambda v: v.scalar_tensor_tensor(gwk2[:], gwk[:], -1e30, gbuf[:, 0:8], op0=ALU.mult, op1=ALU.add),
                 waits=[c_p3v.mark()], inc=(c_p3v, 1))
            p.op("vector", lambda v: v.reduce_max(gb2[:, 5:6], gwk2[:], axis=AX.X),
                 waits=[c_p3v.mark()], inc=(c_p3v, 1))
            p.op("vector", lambda v: v.tensor_scalar(gwk[:], gbuf[:, 0:8], gb2[:, 5:6], None, op0=ALU.is_ge),
                 waits=[c_p3v.mark()], inc=(c_p3v, 1))
            p.op("vector", lambda v: v.tensor_tensor(gwk2[:], gwk[:], gbuf[:, 0:8], op=ALU.mult),
                 waits=[c_p3v.mark()], inc=(c_p3v, 1))
            p.op("vector", lambda v, ts=ts: v.tensor_scalar(cw_all[:, ts, :], gwk2[:], gb2[:, 3:4], None, op0=ALU.mult),
                 waits=[c_p3v.mark()], inc=(c_p3v, 1))
            cw_done = c_p3v.mark()
            if debug:
                p.op("scalar", lambda s, ts=ts: s.dma_start(out=cw_sc[:, ts, :], in_=cw_all[:, ts, :]),
                     waits=[cw_done], inc=(c_scw, 16))
            w = [cw_done]
            if pcwt_free is not None:
                w.append(pcwt_free)
            p.op("tensor", lambda t, ts=ts: t.transpose(pcwt[:], cw_all[:, ts, :], idf[:]),
                 waits=w, inc=(c_p3tr, 1))
            p.op("vector", lambda v, ts=ts: v.tensor_copy(cwT[:, ts * 128:(ts + 1) * 128], pcwt[:]),
                 waits=[c_p3tr.mark()], inc=(c_p3v, 1))
            pcwt_free = c_p3v.mark()

        p.barrier("end_p3", extra_waits=[c_sx1[0].mark(), c_sx1[1].mark(),
                                         c_sln[0].mark(), c_sln[1].mark(), c_scw.mark(),
                                         c_otl[0].mark(), c_otl[1].mark(),
                                         c_xtl[0].mark(), c_xtl[1].mark(), c_wpl.mark()])
        S3.close()

        S5 = ES.enter_context(contextlib.ExitStack())
        sb5 = lambda *a: S5.enter_context(nc.sbuf_tensor(*a))
        ps5 = lambda *a: S5.enter_context(nc.psum_tensor(*a))
        lnxg = sb5("lnxg", [128, 8, GT], BF16)
        x1g_ = [sb5(f"x1g{i}", [128, C], F32) for i in range(2)]
        w1s_ = [sb5(f"w1s{i}", [128, 8, 512], BF16) for i in range(2)]
        w2h_ = [sb5(f"w2h{i}", [128, 16, C], BF16) for i in range(2)]
        ht_ = [sb5(f"ht{i}", [128, 16, GT], BF16) for i in range(1)]
        y_acc = sb5("y_acc", [128, 8, C], F32)
        php = [ps5(f"php{i}", [128, GT], F32) for i in range(2)]
        pyp = [ps5(f"pyp{i}", [128, C], F32) for i in range(2)]

        c_w1 = [p.ctr(f"w1_{i}") for i in range(2)]
        c_w2 = [p.ctr(f"w2_{i}") for i in range(2)]
        c_lnl = p.ctr("lnx_ld")
        c_x1g = [p.ctr("x1g0"), p.ctr("x1g1")]
        c_ffm = p.ctr("ffm")
        c_gel = p.ctr("gel")
        c_yev = p.ctr("yev")
        c_outs = [p.ctr(f"outs{i}") for i in range(8)]

        w1pool = Pool(w1s_, c_w1)
        w2pool = Pool(w2h_, c_w2)
        htpool = Pool(ht_)
        hpool = Pool(php)
        ypool = Pool(pyp)
        x1gpool = Pool(x1g_, c_x1g)
        ylast = [None] * 8
        outlast = [None] * 8
        lnxg_free = None

        for g in range(NG):
            w = [lnxg_free] if lnxg_free is not None else []
            p.op("sync", lambda sy, g=g: sy.dma_start(
                out=lnxg[:], in_=lnx_sc[:, :, g * GT:(g + 1) * GT]),
                waits=w, inc=(c_lnl, 16))
            lnx_ld = c_lnl.mark()
            for ts in range(8):
                x1l, xw, xi, xcc = x1gpool.acquire()
                p.op("sync", lambda sy, g=g, ts=ts, x1l=x1l: sy.dma_start(out=x1l[:], in_=x1v[g * 8 + ts]),
                     waits=xw, inc=(xcc, 16))
                x1m = xcc.mark()
                yp, yw, yi, _ = ypool.acquire()
                tok = (g * 8 + ts) * 128
                for half in range(2):
                    p.op("tensor", lambda t, half=half, yp=yp, tok=tok: t.matmul(
                        yp[:, half * 512:(half + 1) * 512], cwT[:, tok:tok + 128],
                        b2row[:, half * 512:(half + 1) * 512], start=True, stop=True),
                        waits=(yw if half == 0 else []), inc=(c_ffm, 1) if half == 1 else None)
                w = [c_ffm.mark(), x1m]
                if outlast[ts] is not None:
                    w.append(outlast[ts])
                if ylast[ts] is not None:
                    w.append(ylast[ts])
                p.op("vector", lambda v, ts=ts, yp=yp, x1l=x1l: v.tensor_tensor(
                    y_acc[:, ts, :], yp[:], x1l[:], op=ALU.add),
                    waits=w, inc=(c_yev, 1))
                ylast[ts] = c_yev.mark()
                ypool.release(yi, c_yev)
                x1gpool.release(xi, c_yev)
            w1_jobs = [(e_, h_ * 4 + si_) for e_ in range(E) for h_ in range(2) for si_ in range(4)]
            w1_fifo = []
            w1_cur = [0]

            def prefetch_w1():
                if w1_cur[0] >= len(w1_jobs):
                    return
                e_, s_ = w1_jobs[w1_cur[0]]
                w1_cur[0] += 1
                w1l, w1w, w1i, w1c = w1pool.acquire()
                w1view = w1_in[e_].rearrange("(cg p) (s f) -> s p cg f", p=128, f=512)
                p.op("gpsimd", lambda gq, w1l=w1l, w1view=w1view, s_=s_: gq.dma_start(
                    out=w1l[:], in_=w1view[s_]),
                    waits=w1w, inc=(w1c, 16))
                w1_fifo.append((w1l, w1c.mark(), w1i))

            prefetch_w1()
            prefetch_w1()
            for e in range(E):
                for half in range(2):
                    htl, htw, hti, _ = htpool.acquire()
                    w2l, w2w, w2i, w2c = w2pool.acquire()
                    w2view = w2_in[e].rearrange("(hh hc p) d -> hh p hc d", hc=16, p=128)
                    p.op("gpsimd", lambda gq, w2l=w2l, w2view=w2view, half=half: gq.dma_start(
                        out=w2l[:], in_=w2view[half]),
                        waits=w2w, inc=(w2c, 16))
                    w2m = w2c.mark()
                    for si in range(4):
                        s = half * 4 + si
                        w1l, w1m, w1i = w1_fifo.pop(0)
                        for j in range(4):
                            hf = s * 4 + j
                            hfh = si * 4 + j
                            hp, hw, hi, _ = hpool.acquire()
                            for cbk in range(8):
                                for th in range(2):
                                    w = []
                                    if cbk == 0 and th == 0:
                                        w = hw + [w1m, lnx_ld]
                                    p.op("tensor", lambda t, hp=hp, w1l=w1l, cbk=cbk, j=j, th=th: t.matmul(
                                        hp[:, th * 512:(th + 1) * 512],
                                        w1l[:, cbk, j * 128:(j + 1) * 128],
                                        lnxg[:, cbk, th * 512:(th + 1) * 512],
                                        start=(cbk == 0), stop=(cbk == 7)),
                                        waits=w,
                                        inc=(c_ffm, 1) if (cbk == 7 and th == 1) else None)
                            w = [c_ffm.mark()]
                            if hfh == 0:
                                w += htw
                            p.op("scalar", lambda sc, htl=htl, hfh=hfh, hp=hp, hf=hf, e=e: sc.activation(
                                htl[:, hfh, :], hp[:], AF.Gelu, bias=b1T[:, hf, e:e + 1], scale=1.0),
                                waits=w, inc=(c_gel, 1))
                            hpool.release(hi, c_gel)
                        w1pool.release(w1i, c_ffm)
                        prefetch_w1()
                    a_done = c_gel.mark()
                    if debug and g == 0 and e == 0 and half == 0:
                        p.op("sync", lambda sy, htl=htl: sy.dma_start(out=ht_sc[:], in_=htl[:]),
                             waits=[a_done], inc=(c_lnl, 16))
                    for ts in range(8):
                        yp, yw, yi, _ = ypool.acquire()
                        for hfh in range(16):
                            for h2 in range(2):
                                w = []
                                if hfh == 0 and h2 == 0:
                                    w = yw + [a_done, w2m]
                                p.op("tensor", lambda t, yp=yp, htl=htl, w2l=w2l, hfh=hfh, h2=h2, ts=ts: t.matmul(
                                    yp[:, h2 * 512:(h2 + 1) * 512],
                                    htl[:, hfh, ts * 128:(ts + 1) * 128],
                                    w2l[:, hfh, h2 * 512:(h2 + 1) * 512],
                                    start=(hfh == 0), stop=(hfh == 15)),
                                    waits=w,
                                    inc=(c_ffm, 1) if (hfh == 15 and h2 == 1) else None)
                        mmd = c_ffm.mark()
                        p.op("vector", lambda v, yp=yp, g=g, ts=ts, e=e: v.scalar_tensor_tensor(
                            y_acc[:, ts, :], yp[:], cw_all[:, g * 8 + ts, e:e + 1],
                            y_acc[:, ts, :], op0=ALU.mult, op1=ALU.add),
                            waits=[mmd, ylast[ts]], inc=(c_yev, 1))
                        ylast[ts] = c_yev.mark()
                        ypool.release(yi, c_yev)
                        if debug and g == 0 and e == 0 and half == 1:
                            p.op("sync", lambda sy, ts=ts: sy.dma_start(
                                out=y0_sc[:, ts, :], in_=y_acc[:, ts, :]),
                                waits=[ylast[ts]], inc=(c_outs[ts], 16))
                            ylast[ts] = (c_yev, c_yev.val)
                            outlast[ts] = c_outs[ts].mark() if outlast[ts] is None else outlast[ts]
                            ylast_dump = c_outs[ts].mark()
                            ylast[ts] = ylast_dump
                        if e == E - 1 and half == 1:
                            p.op("sync", lambda sy, g=g, ts=ts: sy.dma_start(
                                out=outv[g * 8 + ts], in_=y_acc[:, ts, :]),
                                waits=[ylast[ts]], inc=(c_outs[ts], 16))
                            outlast[ts] = c_outs[ts].mark()
                    htpool.release(hti, c_ffm)
                    w2pool.release(w2i, c_ffm)
            lnxg_free = c_ffm.mark()

        for ts in range(8):
            p.wait("sync", (c_outs[ts], c_outs[ts].val))
        with nc.Block() as block:
            p.build(block)
    return nc


_IDF = np.eye(128, dtype=np.float32)
_IDB = np.eye(128).astype(ml_dtypes.bfloat16)


def kernel(L0_attn_x, L0_attn_sin, L0_attn_cos, L0_attn_w_qkv, L0_attn_b_qkv,
           L0_attn_w_proj, L0_attn_b_proj, L0_ffn_w_gate, L0_ffn_W1, L0_ffn_b1,
           L0_ffn_W2, L0_ffn_b2):
    x = np.ascontiguousarray(np.asarray(L0_attn_x, dtype=np.float32))
    common = {
        "sin": np.ascontiguousarray(np.asarray(L0_attn_sin, dtype=np.float32)),
        "cos": np.ascontiguousarray(np.asarray(L0_attn_cos, dtype=np.float32)),
        "wqkv": np.ascontiguousarray(np.asarray(L0_attn_w_qkv, dtype=np.float32)),
        "bqkv": np.ascontiguousarray(np.asarray(L0_attn_b_qkv, dtype=np.float32)),
        "wproj": np.ascontiguousarray(np.asarray(L0_attn_w_proj, dtype=np.float32)),
        "bproj": np.ascontiguousarray(np.asarray(L0_attn_b_proj, dtype=np.float32)),
        "wg": np.ascontiguousarray(np.asarray(L0_ffn_w_gate, dtype=np.float32)),
        "w1": np.ascontiguousarray(np.asarray(L0_ffn_W1, dtype=np.float32)),
        "b1": np.ascontiguousarray(np.asarray(L0_ffn_b1, dtype=np.float32)),
        "w2": np.ascontiguousarray(np.asarray(L0_ffn_W2, dtype=np.float32)),
        "b2": np.ascontiguousarray(np.asarray(L0_ffn_b2, dtype=np.float32)),
        "idf": _IDF,
        "idb": _IDB,
    }
    nc = build_nc(debug=False)
    in_maps = [dict(common, x=np.ascontiguousarray(x[b])) for b in range(8)]
    res = run_bass_kernel_spmd(nc, in_maps, core_ids=list(range(8)))
    return np.stack([np.asarray(res.results[b]["out"]) for b in range(8)], axis=0)


# version 02
# speedup vs baseline: 1.0359x; 1.0359x over previous
import contextlib
import numpy as np
import ml_dtypes
import concourse.bass as bass
import concourse.mybir as mybir
from concourse.bass_utils import run_bass_kernel_spmd

F32 = mybir.dt.float32
BF16 = mybir.dt.bfloat16
AF = mybir.ActivationFunctionType
ALU = mybir.AluOpType
AX = mybir.AxisListType

T = 4096
C = 1024
H = 16
DH = 64
NPAIR = 8
NROPE = 1369
PREFIX = T - NROPE
E = 8
HID = 4096
NG = 4
GT = T // NG
NTS = T // 128
QB = 512
NQB = T // QB
KT = T // 128

ENGINES = ("tensor", "vector", "scalar", "gpsimd", "sync")


class Ctr:
    __slots__ = ("name", "sem", "val")

    def __init__(self, name, sem):
        self.name = name
        self.sem = sem
        self.val = 0

    def mark(self):
        return (self, self.val)


class Prog:
    def __init__(self, nc, stack):
        self.nc = nc
        self.stack = stack
        self.stream = {e: [] for e in ENGINES}

    def ctr(self, name):
        sem = self.stack.enter_context(self.nc.semaphore(name))
        return Ctr(name, sem)

    def op(self, eng, fn, waits=(), inc=None):
        wl = []
        for w in waits:
            if w is None:
                continue
            c, v = w if isinstance(w, tuple) else (w, w.val)
            assert v <= c.val, f"wait on future value {c.name} {v} > {c.val}"
            if v > 0:
                wl.append((c, v))
        if inc is not None:
            c, n = inc
            c.val += n
        self.stream[eng].append((fn, tuple(wl), inc))

    def wait(self, eng, *waits):
        self.op(eng, None, waits=waits)

    def barrier(self, name, extra_waits=()):
        b = self.ctr(name)
        for e in ENGINES:
            w = tuple(extra_waits) if e == "sync" else ()
            self.op(e, (lambda eng: eng.drain()), waits=w, inc=(b, 1))
        for e in ENGINES:
            self.wait(e, (b, len(ENGINES)))
        return b

    def build(self, block):
        for e in ENGINES:
            self._replay(getattr(block, e), self.stream[e])

    def _replay(self, deco, ops):
        @deco
        def _(eng):
            for fn, waits, inc in ops:
                for c, v in waits:
                    eng.wait_ge(c.sem, v)
                ins = fn(eng) if fn is not None else None
                if inc is not None:
                    c, n = inc
                    if ins is None:
                        ins = eng.nop()
                    ins.then_inc(c.sem, n)


class Pool:

    def __init__(self, slots, ctrs=None):
        self.slots = list(slots)
        self.ctrs = ctrs
        self.markers = [None] * len(self.slots)
        self.i = 0

    def acquire(self):
        s = self.i % len(self.slots)
        self.i += 1
        w = self.markers[s]
        ctr = self.ctrs[s] if self.ctrs is not None else None
        return self.slots[s], ([w] if w is not None else []), s, ctr

    def release(self, s, ctr):
        self.markers[s] = ctr.mark()


def build_nc(debug=False):
    nc = bass.Bass("TRN2", target_bir_lowering=False)
    dbg_kind = "ExternalOutput" if debug else "Internal"

    x_in = nc.dram_tensor("x", [T, C], F32, kind="ExternalInput").ap()
    sin_in = nc.dram_tensor("sin", [NROPE, DH], F32, kind="ExternalInput").ap()
    cos_in = nc.dram_tensor("cos", [NROPE, DH], F32, kind="ExternalInput").ap()
    wqkv_in = nc.dram_tensor("wqkv", [C, 3 * C], F32, kind="ExternalInput").ap()
    bqkv_in = nc.dram_tensor("bqkv", [3 * C], F32, kind="ExternalInput").ap()
    wproj_in = nc.dram_tensor("wproj", [C, C], F32, kind="ExternalInput").ap()
    bproj_in = nc.dram_tensor("bproj", [C], F32, kind="ExternalInput").ap()
    wg_in = nc.dram_tensor("wg", [E, C], F32, kind="ExternalInput").ap()
    w1_in = nc.dram_tensor("w1", [E, C, HID], F32, kind="ExternalInput").ap()
    b1_in = nc.dram_tensor("b1", [E, HID], F32, kind="ExternalInput").ap()
    w2_in = nc.dram_tensor("w2", [E, HID, C], F32, kind="ExternalInput").ap()
    b2_in = nc.dram_tensor("b2", [E, C], F32, kind="ExternalInput").ap()
    idf_in = nc.dram_tensor("idf", [128, 128], F32, kind="ExternalInput").ap()
    idb_in = nc.dram_tensor("idb", [128, 128], BF16, kind="ExternalInput").ap()
    out = nc.dram_tensor("out", [T, C], F32, kind="ExternalOutput").ap()

    qt_sc = nc.dram_tensor("qt_sc", [NPAIR, 128, T], BF16, kind=dbg_kind).ap()
    kt_sc = nc.dram_tensor("kt_sc", [NPAIR, 128, T], BF16, kind=dbg_kind).ap()
    von_sc = nc.dram_tensor("von_sc", [NPAIR, 128, 2 * KT * 65], BF16, kind=dbg_kind).ap()
    ot_sc = nc.dram_tensor("ot_sc", [NPAIR, 128, T], BF16, kind=dbg_kind).ap()
    x1_sc = nc.dram_tensor("x1_sc", [T, C], F32, kind=dbg_kind).ap()
    lnx_sc = nc.dram_tensor("lnx_sc", [128, 8, T], BF16, kind=dbg_kind).ap()
    cw_sc = nc.dram_tensor("cw_sc", [128, NTS, E], F32, kind=dbg_kind).ap()
    ht_sc = nc.dram_tensor("ht_sc", [128, 16, GT], BF16, kind=dbg_kind).ap()
    y0_sc = nc.dram_tensor("y0_sc", [128, 8, C], F32, kind=dbg_kind).ap()

    ES = contextlib.ExitStack()
    with ES:
        sb = lambda *a: ES.enter_context(nc.sbuf_tensor(*a))
        psum = lambda *a: ES.enter_context(nc.psum_tensor(*a))
        p = Prog(nc, ES)

        idf = sb("idf_s", [128, 128], F32)
        idb = sb("idb_s", [128, 128], BF16)
        ones_f = sb("ones_f", [1, 128], F32)
        ones_b = sb("ones_b", [1, 128], BF16)
        maskh = sb("maskh", [128, 2, 128], BF16)
        bq_sb = sb("bq_sb", [128, 24], F32)
        bprow = sb("bprow", [1, C], BF16)
        b2row = sb("b2row", [E, C], F32)
        b1T = sb("b1T", [128, 32, E], F32)
        wgT = sb("wgT", [128, 8, E], F32)
        sinF = sb("sinF", [128, NROPE], BF16)
        cosF = sb("cosF", [128, NROPE], BF16)
        cw_all = sb("cw_all", [128, NTS, E], F32)
        cwT = sb("cwT", [8, T], F32)
        epsc = sb("epsc", [128, 1], F32)
        junk = sb("junk", [128, 1024], F32)

        c_ic = p.ctr("ic")
        c_icv = p.ctr("icv")
        c_ica = p.ctr("ica")

        SC = ES.enter_context(contextlib.ExitStack())
        sbc = lambda *a: SC.enter_context(nc.sbuf_tensor(*a))
        bq_stage = sbc("bq_stage", [24, 128], F32)
        b1_stage = sbc("b1_stage", [8, HID], F32)
        wg_stage = sbc("wg_stage", [8, C], F32)
        sin_stage = sbc("sin_stage", [128, 11, DH], F32)
        cos_stage = sbc("cos_stage", [128, 11, DH], F32)

        p.op("sync", lambda sy: sy.dma_start(out=idf[:], in_=idf_in), inc=(c_ic, 16))
        p.op("sync", lambda sy: sy.dma_start(out=idb[:], in_=idb_in), inc=(c_ic, 16))
        p.op("sync", lambda sy: sy.dma_start(out=bq_stage[:], in_=bqkv_in.rearrange("(a b) -> a b", b=128)[0:24]), inc=(c_ic, 16))
        p.op("sync", lambda sy: sy.dma_start(out=b2row[:], in_=b2_in), inc=(c_ic, 16))
        p.op("sync", lambda sy: sy.dma_start(out=b1_stage[:], in_=b1_in), inc=(c_ic, 16))
        p.op("sync", lambda sy: sy.dma_start(out=wg_stage[:], in_=wg_in), inc=(c_ic, 16))
        p.op("sync", lambda sy: sy.dma_start(out=sin_stage[0:128, 0:10, :], in_=sin_in[0:1280].rearrange("(n p) d -> p n d", p=128)), inc=(c_ic, 16))
        p.op("sync", lambda sy: sy.dma_start(out=sin_stage[0:89, 10, :], in_=sin_in[1280:NROPE, :]), inc=(c_ic, 16))
        p.op("sync", lambda sy: sy.dma_start(out=cos_stage[0:128, 0:10, :], in_=cos_in[0:1280].rearrange("(n p) d -> p n d", p=128)), inc=(c_ic, 16))
        p.op("sync", lambda sy: sy.dma_start(out=cos_stage[0:89, 10, :], in_=cos_in[1280:NROPE, :]), inc=(c_ic, 16))
        p.op("gpsimd", lambda g: g.dma_start(out=bprow[:], in_=bproj_in.rearrange("(a b) -> a b", a=1)), inc=(c_ic, 16))

        p.op("vector", lambda v: v.memset(ones_f[:], 1.0), inc=(c_icv, 1))
        p.op("vector", lambda v: v.memset(ones_b[:], 1.0), inc=(c_icv, 1))
        p.op("vector", lambda v: v.memset(maskh[:], 0.0), inc=(c_icv, 1))
        p.op("vector", lambda v: v.memset(maskh[0:64, 0, :], 1.0), waits=[c_icv.mark()], inc=(c_icv, 1))
        p.op("vector", lambda v: v.memset(maskh[64:128, 1, :], 1.0), waits=[c_icv.mark()], inc=(c_icv, 1))
        p.op("vector", lambda v: v.memset(epsc[:], 1e-5), inc=(c_icv, 1))

        pconst = SC.enter_context(nc.psum_tensor("pconst", [128, 512], F32))
        pconst2 = SC.enter_context(nc.psum_tensor("pconst2", [64, 1408], F32))
        ld_all = c_ic.mark()
        p.op("tensor", lambda t: t.transpose(pconst[:, 0:24], bq_stage[:], idf[0:24, 0:24]),
             waits=[ld_all], inc=(c_ica, 1))
        p.op("vector", lambda v: v.tensor_copy(bq_sb[:], pconst[:, 0:24]),
             waits=[c_ica.mark()], inc=(c_icv, 1))
        for hf in range(32):
            p.op("tensor", lambda t, hf=hf: t.transpose(pconst[:, 32 + hf * 8:40 + hf * 8], b1_stage[:, hf * 128:(hf + 1) * 128], idf[0:8, 0:8]),
                 waits=[ld_all, c_icv.mark()], inc=(c_ica, 1))
        p.op("vector", lambda v: v.tensor_copy(b1T[:], pconst[:, 32:32 + 256].rearrange("p (hf e) -> p hf e", e=8)),
             waits=[c_ica.mark()], inc=(c_icv, 1))
        for cc in range(8):
            p.op("tensor", lambda t, cc=cc: t.transpose(pconst[:, 320 + cc * 8:328 + cc * 8], wg_stage[:, cc * 128:(cc + 1) * 128], idf[0:8, 0:8]),
                 waits=[ld_all, c_icv.mark()], inc=(c_ica, 1))
        p.op("vector", lambda v: v.tensor_copy(wgT[:], pconst[:, 320:320 + 64].rearrange("p (cc e) -> p cc e", e=8)),
             waits=[c_ica.mark()], inc=(c_icv, 1))
        for stage, dstF, negate in ((sin_stage, sinF, True), (cos_stage, cosF, False)):
            for i in range(11):
                n = 128 if i < 10 else 89
                p.op("tensor", lambda t, i=i, n=n, stage=stage: t.transpose(
                    pconst2[:, i * 128:i * 128 + n], stage[0:n, i, :], idf[0:n, 0:n]),
                    waits=[ld_all, c_icv.mark()], inc=(c_ica, 1))
            p.op("vector", lambda v, dstF=dstF: v.tensor_copy(dstF[0:64, :], pconst2[:, 0:NROPE]),
                 waits=[c_ica.mark()], inc=(c_icv, 1))
            if negate:
                p.op("vector", lambda v, dstF=dstF: v.tensor_scalar(
                    dstF[0:32, :], dstF[0:32, :], -1.0, None, op0=ALU.mult),
                    waits=[c_icv.mark()], inc=(c_icv, 1))
            p.op("scalar", lambda s, dstF=dstF: s.dma_start(out=dstF[64:128, :], in_=dstF[0:64, :]),
                 waits=[c_icv.mark()], inc=(c_ic, 16))
        consts_done = [c_ic.mark(), c_icv.mark(), c_ica.mark()]
        SC.close()

        S01 = ES.enter_context(contextlib.ExitStack())
        sb1 = lambda *a: S01.enter_context(nc.sbuf_tensor(*a))
        ps1 = lambda *a: S01.enter_context(nc.psum_tensor(*a))
        xnT = sb1("xnT", [128, 8, T], BF16)
        wqkv_bf = sb1("wqkv_bf", [128, 8, 3 * C], BF16)
        xt = [sb1(f"xt{i}", [128, C], F32) for i in range(2)]
        xnb = [sb1(f"xn{i}", [128, C], BF16) for i in range(2)]
        stat = sb1("stat", [128, 8], F32)
        rstd2 = sb1("rstd2", [128, 2], F32)
        ptr = [ps1("ptr0", [128, 128], BF16), ps1("ptr1", [128, 128], BF16)]

        c_x = [p.ctr("x_ld0"), p.ctr("x_ld1")]
        c_wq = p.ctr("wq_ld")
        c_lnv = p.ctr("lnv")
        c_lna = p.ctr("lna")
        c_tr = p.ctr("tr")
        c_xnt = p.ctr("xnt")

        wv = wqkv_in.rearrange("(cc p) f -> cc p f", p=128)
        for cc in range(8):
            p.op("gpsimd", lambda g, cc=cc: g.dma_start(out=wqkv_bf[:, cc, :], in_=wv[cc]),
                 waits=(consts_done if cc == 0 else ()), inc=(c_wq, 16))
        wq_done = c_wq.mark()

        xview = x_in.rearrange("(n p) c -> n p c", p=128)
        xtile_free = [None, None]
        xn_free = [None, None]
        for ts in range(NTS):
            b = ts % 2
            w = [consts_done[0], consts_done[1], consts_done[2]] if ts < 2 else []
            if xtile_free[b] is not None:
                w.append(xtile_free[b])
            p.op("sync", lambda sy, ts=ts, b=b: sy.dma_start(out=xt[b][:], in_=xview[ts]),
                 waits=w, inc=(c_x[b], 16))
            xld = c_x[b].mark()
            o = 4 * b
            p.op("scalar", lambda s, b=b, o=o: s.activation(junk[:], xt[b][:], AF.Identity, bias=0.0, scale=1.0, accum_out=stat[:, o:o + 1]),
                 waits=[xld, c_lnv.mark()], inc=(c_lna, 1))
            p.op("scalar", lambda s, b=b, o=o: s.activation(junk[:], xt[b][:], AF.Square, bias=0.0, scale=1.0, accum_out=stat[:, o + 1:o + 2]),
                 waits=[c_lna.mark()], inc=(c_lna, 1))
            acc_done = c_lna.mark()
            p.op("vector", lambda v, o=o: v.tensor_scalar(stat[:, o + 2:o + 3], stat[:, o:o + 1], 1.0 / C, None, op0=ALU.mult),
                 waits=[acc_done], inc=(c_lnv, 1))
            p.op("vector", lambda v, o=o: v.tensor_tensor(stat[:, o + 3:o + 4], stat[:, o + 2:o + 3], stat[:, o + 2:o + 3], op=ALU.mult),
                 waits=[c_lnv.mark()], inc=(c_lnv, 1))
            p.op("vector", lambda v, o=o: v.scalar_tensor_tensor(stat[:, o + 3:o + 4], stat[:, o + 1:o + 2], 1.0 / C, stat[:, o + 3:o + 4], op0=ALU.mult, op1=ALU.subtract),
                 waits=[c_lnv.mark()], inc=(c_lnv, 1))
            p.op("scalar", lambda s, o=o: s.activation(stat[:, o + 3:o + 4], stat[:, o + 3:o + 4], AF.Ln, bias=epsc[:], scale=1.0),
                 waits=[c_lnv.mark()], inc=(c_lna, 1))
            p.op("scalar", lambda s, b=b, o=o: s.activation(rstd2[:, b:b + 1], stat[:, o + 3:o + 4], AF.Exp, bias=0.0, scale=-0.5),
                 waits=[c_lna.mark()], inc=(c_lna, 1))
            w2_ = [c_lna.mark()]
            if xn_free[b] is not None:
                w2_.append(xn_free[b])
            p.op("vector", lambda v, b=b, o=o: v.tensor_scalar(xnb[b][:], xt[b][:], stat[:, o + 2:o + 3], rstd2[:, b:b + 1], op0=ALU.subtract, op1=ALU.mult),
                 waits=w2_, inc=(c_lnv, 1))
            xtile_free[b] = c_lnv.mark()
            xn_done = c_lnv.mark()
            for cc in range(8):
                pt = ptr[cc % 2]
                tw = [xn_done]
                gidx = ts * 8 + cc
                if gidx >= 2:
                    tw.append((c_xnt, gidx - 1))
                p.op("tensor", lambda t, b=b, cc=cc, pt=pt: t.transpose(pt[:], xnb[b][:, cc * 128:(cc + 1) * 128], idb[:]),
                     waits=tw, inc=(c_tr, 1))
                p.op("vector", lambda v, ts=ts, cc=cc, pt=pt: v.tensor_copy(xnT[:, cc, ts * 128:(ts + 1) * 128], pt[:]),
                     waits=[c_tr.mark()], inc=(c_xnt, 1))
            xn_free[b] = c_xnt.mark()
        xnT_done = c_xnt.mark()

        qt_buf = sb1("qt_buf", [128, T], BF16)
        kt_buf = sb1("kt_buf", [128, T], BF16)
        vt_buf = sb1("vt_buf", [128, T], BF16)
        von_buf = sb1("von_buf", [128, 2, KT, 65], BF16)
        rot = sb1("rot", [128, NROPE], BF16)
        rtmp = sb1("rtmp", [128, NROPE], BF16)
        pqk = [ps1(f"pqk{i}", [128, 512], F32) for i in range(4)]
        pvt = [ps1(f"pvt{i}", [128, 128], BF16) for i in range(2)]

        c_p1mm = p.ctr("p1mm")
        c_p1ev = p.ctr("p1ev")
        c_rope = p.ctr("rope")
        c_spq = p.ctr("spq")
        c_spk = p.ctr("spk")
        c_spv = p.ctr("spv")
        c_vcp = p.ctr("vcp")

        qk_pool = Pool(pqk)
        vt_pool = Pool(pvt)

        def proj_chunk(fc, dst_buf, buf_free):
            for tb in range(8):
                pb, pw, ps_, _ = qk_pool.acquire()
                for cbk in range(8):
                    w = []
                    if cbk == 0:
                        w += pw
                        if tb == 0:
                            w += [xnT_done, wq_done]
                            if buf_free is not None:
                                w.append(buf_free)
                    p.op("tensor", lambda t, fc=fc, tb=tb, cbk=cbk, pb=pb: t.matmul(
                        pb[:], wqkv_bf[:, cbk, fc * 128:(fc + 1) * 128],
                        xnT[:, cbk, tb * 512:(tb + 1) * 512],
                        start=(cbk == 0), stop=(cbk == 7)),
                        waits=w, inc=(c_p1mm, 1) if cbk == 7 else None)
                p.op("scalar", lambda s, fc=fc, tb=tb, pb=pb, dst_buf=dst_buf: s.activation(
                    dst_buf[:, tb * 512:(tb + 1) * 512], pb[:], AF.Identity,
                    bias=bq_sb[:, fc:fc + 1], scale=1.0),
                    waits=[c_p1mm.mark()], inc=(c_p1ev, 1))
                qk_pool.release(ps_, c_p1ev)
            return c_p1ev.mark()

        def rope_inplace(dst_buf, ev_done):
            for (a, b_) in ((0, 32), (32, 0), (64, 96), (96, 64)):
                p.op("scalar", lambda s, a=a, b_=b_, dst_buf=dst_buf: s.dma_start(
                    out=rot[a:a + 32, :], in_=dst_buf[b_:b_ + 32, PREFIX:T]),
                    waits=[ev_done, c_rope.mark()], inc=(c_rope, 16))
            p.op("vector", lambda v: v.tensor_tensor(rtmp[:], rot[:], sinF[:], op=ALU.mult),
                 waits=[c_rope.mark()], inc=(c_rope, 1))
            p.op("vector", lambda v, dst_buf=dst_buf: v.tensor_tensor(rot[:], dst_buf[:, PREFIX:T], cosF[:], op=ALU.mult),
                 waits=[c_rope.mark()], inc=(c_rope, 1))
            p.op("vector", lambda v, dst_buf=dst_buf: v.tensor_tensor(dst_buf[:, PREFIX:T], rot[:], rtmp[:], op=ALU.add),
                 waits=[c_rope.mark()], inc=(c_rope, 1))
            return c_rope.mark()

        pq_free = pk_free = pv_free = pvon_free = None
        for pair in range(NPAIR):
            ev = proj_chunk(pair, qt_buf, pq_free)
            rp = rope_inplace(qt_buf, ev)
            p.op("scalar", lambda s, pair=pair: s.dma_start(out=qt_sc[pair], in_=qt_buf[:]),
                 waits=[rp], inc=(c_spq, 16))
            pq_free = c_spq.mark()
            ev = proj_chunk(8 + pair, kt_buf, pk_free)
            rp = rope_inplace(kt_buf, ev)
            p.op("scalar", lambda s, pair=pair: s.dma_start(out=kt_sc[pair], in_=kt_buf[:]),
                 waits=[rp], inc=(c_spk, 16))
            pk_free = c_spk.mark()
            ev = proj_chunk(16 + pair, vt_buf, pv_free)
            w = [pvon_free] if pvon_free is not None else []
            p.op("vector", lambda v: v.memset(von_buf[:, :, :, 64:65], 1.0),
                 waits=w + [c_vcp.mark()], inc=(c_vcp, 1))
            for kt in range(KT):
                pt, pw, ps_, _ = vt_pool.acquire()
                p.op("tensor", lambda t, kt=kt, pt=pt: t.transpose(
                    pt[:], vt_buf[:, kt * 128:(kt + 1) * 128], idb[:]),
                    waits=pw + ([ev] if kt == 0 else []), inc=(c_tr, 1))
                trm = c_tr.mark()
                p.op("vector", lambda v, kt=kt, pt=pt: v.tensor_copy(von_buf[:, 0, kt, 0:64], pt[:, 0:64]),
                     waits=[trm], inc=(c_vcp, 1))
                p.op("vector", lambda v, kt=kt, pt=pt: v.tensor_copy(von_buf[:, 1, kt, 0:64], pt[:, 64:128]),
                     waits=[trm], inc=(c_vcp, 1))
                vt_pool.release(ps_, c_vcp)
            pv_free = c_tr.mark()
            p.op("scalar", lambda s, pair=pair: s.dma_start(
                out=von_sc[pair], in_=von_buf[:].rearrange("p h kt f -> p (h kt f)")),
                waits=[c_vcp.mark()], inc=(c_spv, 16))
            pvon_free = c_spv.mark()

        p.barrier("end_p1", extra_waits=[c_spq.mark(), c_spk.mark(), c_spv.mark(),
                                         c_x[0].mark(), c_x[1].mark(), c_rope.mark(),
                                         c_ic.mark(), c_wq.mark()])
        S01.close()

        S2 = ES.enter_context(contextlib.ExitStack())
        sb2 = lambda *a: S2.enter_context(nc.sbuf_tensor(*a))
        ps2 = lambda *a: S2.enter_context(nc.psum_tensor(*a))
        qs_ = [sb2(f"qs{i}", [128, T], BF16) for i in range(2)]
        ks_ = [sb2(f"ks{i}", [128, T], BF16) for i in range(2)]
        vo_ = [sb2(f"vo{i}", [128, 2, KT, 65], BF16) for i in range(2)]
        strips = [sb2(f"strip{i}", [128, KT, QB], BF16) for i in range(2)]
        otu_ = [sb2(f"otu{i}", [65, QB], F32) for i in range(2)]
        dq_ = [sb2(f"dq{i}", [1, QB], F32) for i in range(2)]
        rq_ = [sb2(f"rq{i}", [1, QB], F32) for i in range(2)]
        otp_ = [sb2(f"otp{i}", [128, T], BF16) for i in range(2)]
        sqb = sb2("sqb", [128, T], BF16)
        nbuf = sb2("nbuf", [128, 16], F32)
        nmc = sb2("nmc", [128, 4], F32)
        s2p = [ps2(f"s2p{i}", [128, 2, 512], F32) for i in range(2)]
        potp = [ps2(f"potp{i}", [65, QB], F32) for i in range(3)]

        c_alq = [p.ctr("alq0"), p.ctr("alq1")]
        c_alk = [p.ctr("alk0"), p.ctr("alk1")]
        c_alv = [p.ctr("alv0"), p.ctr("alv1")]
        c_smm = p.ctr("smm")
        c_ex = p.ctr("ex")
        c_pv = p.ctr("pv")
        c_oev = p.ctr("oev")
        c_dqm = [p.ctr("dqm0"), p.ctr("dqm1")]
        c_rq = p.ctr("rqm")
        c_bc = p.ctr("bc")
        c_fin = p.ctr("fin")
        c_nrm = p.ctr("nrm")
        c_osp = [p.ctr("osp0"), p.ctr("osp1")]

        qpool = Pool(qs_, c_alq)
        kpool = Pool(ks_, c_alk)
        vpool = Pool(vo_, c_alv)
        spool = Pool(strips)
        opool = Pool(otu_)
        dqpool = Pool(dq_, c_dqm)
        rqpool = Pool(rq_)
        otpool = Pool(otp_)
        s2pool = Pool(s2p)
        potpool = Pool(potp)

        def load_pair(pair):
            qsl, qw, qi, qc = qpool.acquire()
            ksl, kw, ki, kc = kpool.acquire()
            vol, vw, vi, vc = vpool.acquire()
            p.op("sync", lambda sy, pair=pair, qsl=qsl: sy.dma_start(out=qsl[:], in_=qt_sc[pair]),
                 waits=qw, inc=(qc, 16))
            p.op("sync", lambda sy, pair=pair, ksl=ksl: sy.dma_start(out=ksl[:], in_=kt_sc[pair]),
                 waits=kw, inc=(kc, 16))
            p.op("sync", lambda sy, pair=pair, vol=vol: sy.dma_start(
                out=vol[:].rearrange("p h kt f -> p (h kt f)"), in_=von_sc[pair]),
                 waits=vw, inc=(vc, 16))
            return (qsl, qi), (ksl, ki), (vol, vi), [qc.mark(), kc.mark(), vc.mark()]

        pair_ex_end = {}
        pend = {0: load_pair(0)}
        pair_state = {}

        def norm_chain(pair, qsl, ksl, ldm):
            nmo = (pair % 2) * 2
            for side, buf in ((0, qsl), (1, ksl)):
                p.op("scalar", lambda s, buf=buf: s.activation(sqb[:], buf[:], AF.Square, bias=0.0, scale=1.0),
                     waits=ldm + [c_nrm.mark()], inc=(c_nrm, 1))
                sq_done = c_nrm.mark()
                for h in range(2):
                    ni = side * 2 + h
                    s2l, s2w, s2i, _ = s2pool.acquire()
                    for blk in range(8):
                        p.op("tensor", lambda t, h=h, blk=blk, s2l=s2l: t.matmul(
                            s2l[:, blk % 2, :], maskh[:, h, :], sqb[:, blk * 512:(blk + 1) * 512],
                            start=True, stop=True),
                            waits=[sq_done, c_nrm.mark()] + (s2w if blk == 0 else []), inc=(c_nrm, 1))
                        p.op("vector", lambda v, ni=ni, blk=blk, s2l=s2l: v.reduce_max(
                            nbuf[:, 8 + ni:9 + ni], s2l[:, blk % 2, :], axis=AX.X),
                            waits=[c_nrm.mark()], inc=(c_nrm, 1))
                        if blk == 0:
                            p.op("vector", lambda v, ni=ni: v.tensor_copy(nbuf[:, ni:ni + 1], nbuf[:, 8 + ni:9 + ni]),
                                 waits=[c_nrm.mark()], inc=(c_nrm, 1))
                        else:
                            p.op("vector", lambda v, ni=ni: v.tensor_tensor(
                                nbuf[:, ni:ni + 1], nbuf[:, ni:ni + 1], nbuf[:, 8 + ni:9 + ni], op=ALU.max),
                                waits=[c_nrm.mark()], inc=(c_nrm, 1))
                    s2pool.release(s2i, c_nrm)
            nmc_war = (c_ex, pair_ex_end[pair - 2]) if pair >= 2 else None
            for h in range(2):
                p.op("vector", lambda v, h=h: v.tensor_tensor(nbuf[:, 12 + h:13 + h], nbuf[:, h:h + 1], nbuf[:, 2 + h:3 + h], op=ALU.mult),
                     waits=[c_nrm.mark()], inc=(c_nrm, 1))
                p.op("scalar", lambda s, h=h: s.activation(nbuf[:, 12 + h:13 + h], nbuf[:, 12 + h:13 + h], AF.Ln, bias=epsc[:], scale=1.0),
                     waits=[c_nrm.mark()], inc=(c_nrm, 1))
                p.op("scalar", lambda s, h=h: s.activation(nbuf[:, 12 + h:13 + h], nbuf[:, 12 + h:13 + h], AF.Exp, bias=0.0, scale=0.5),
                     waits=[c_nrm.mark()], inc=(c_nrm, 1))
                p.op("vector", lambda v, h=h, nmo=nmo: v.tensor_scalar(nmc[:, nmo + h:nmo + h + 1], nbuf[:, 12 + h:13 + h], -0.127575, None, op0=ALU.mult),
                     waits=[c_nrm.mark(), nmc_war], inc=(c_nrm, 1))
            return c_nrm.mark()

        items = [(pair, h, qb) for pair in range(NPAIR) for h in range(2) for qb in range(NQB)]
        st_info = {}

        def stage_S(idx):
            pair, h, qb = items[idx]
            ps_ = pair_state[pair]
            qsl, ksl = ps_["qsl"], ps_["ksl"]
            nmo = (pair % 2) * 2
            stl, stw, sti, _ = spool.acquire()
            ex0 = c_ex.val
            for ktg in range(KT // 2):
                s2l, s2w, s2i, _ = s2pool.acquire()
                for sub in range(2):
                    kt = 2 * ktg + sub
                    w = list(s2w) if sub == 0 else []
                    if ktg == 0 and sub == 0:
                        w += [ps_["nm_done"]] + stw
                    p.op("tensor", lambda t, h=h, qb=qb, kt=kt, sub=sub, s2l=s2l, ksl=ksl, qsl=qsl: t.matmul(
                        s2l[:, sub, :], ksl[h * 64:(h + 1) * 64, kt * 128:(kt + 1) * 128],
                        qsl[h * 64:(h + 1) * 64, qb * QB:(qb + 1) * QB],
                        start=True, stop=True),
                        waits=w, inc=(c_smm, 1) if sub == 1 else None)
                p.op("scalar", lambda s, h=h, ktg=ktg, s2l=s2l, stl=stl, nmo=nmo: s.activation(
                    stl[:, 2 * ktg:2 * ktg + 2, :], s2l[:], AF.Exp,
                    bias=nmc[:, nmo + h:nmo + h + 1], scale=0.125),
                    waits=[c_smm.mark()], inc=(c_ex, 1))
                s2pool.release(s2i, c_ex)
            st_info[idx] = {"stl": stl, "sti": sti, "ex_done": c_ex.mark()}

        def stage_PV(idx):
            pair, h, qb = items[idx]
            ps_ = pair_state[pair]
            vol = ps_["vol"]
            si = st_info[idx]
            stl = si["stl"]
            pol, pow_, poi, _ = potpool.acquire()
            for kt in range(KT):
                w = []
                if kt == 0:
                    w = [si["ex_done"]] + pow_
                p.op("tensor", lambda t, h=h, kt=kt, pol=pol, vol=vol, stl=stl: t.matmul(
                    pol[:], vol[:, h, kt, :], stl[:, kt, :],
                    start=(kt == 0), stop=(kt == KT - 1)),
                    waits=w, inc=(c_pv, 1) if kt == KT - 1 else None)
            spool.release(si["sti"], c_pv)
            pv_done = c_pv.mark()
            if (h, qb) == (1, NQB - 1):
                qpool.release(ps_["qi"], c_pv)
                kpool.release(ps_["ki"], c_pv)
                vpool.release(ps_["vi"], c_pv)
            oul, ouw, oui, _ = opool.acquire()
            p.op("vector", lambda v, oul=oul, pol=pol: v.tensor_copy(oul[:], pol[:]),
                 waits=[pv_done] + ouw, inc=(c_oev, 1))
            ev_done = c_oev.mark()
            dql, dqw, dqi, dqc = dqpool.acquire()
            p.op("scalar", lambda s, dql=dql, oul=oul: s.dma_start(out=dql[:], in_=oul[64:65, :]),
                 waits=[ev_done] + dqw, inc=(dqc, 16))
            rql, rqw, rqi, _ = rqpool.acquire()
            p.op("vector", lambda v, rql=rql, dql=dql: v.reciprocal(rql[:], dql[:]),
                 waits=[dqc.mark()] + rqw, inc=(c_rq, 1))
            dqpool.release(dqi, c_rq)
            si.update(pol=pol, poi=poi, oul=oul, oui=oui, rql=rql, rqi=rqi,
                      ev_done=ev_done, rq_done=c_rq.mark())

        def stage_NORM(idx):
            pair, h, qb = items[idx]
            ps_ = pair_state[pair]
            si = st_info.pop(idx)
            pol, oul, rql = si["pol"], si["oul"], si["rql"]
            p.op("tensor", lambda t, pol=pol, rql=rql: t.matmul(
                pol[0:64, :], ones_f[0:1, 0:64], rql[:], start=True, stop=True),
                waits=[si["rq_done"], si["ev_done"]], inc=(c_bc, 1))
            rqpool.release(si["rqi"], c_bc)
            otsl = ps_["otsl"]
            p.op("vector", lambda v, h=h, qb=qb, otsl=otsl, oul=oul, pol=pol: v.tensor_tensor(
                otsl[h * 64:(h + 1) * 64, qb * QB:(qb + 1) * QB],
                oul[0:64, :], pol[0:64, :], op=ALU.mult),
                waits=[c_bc.mark()] + (ps_["otw"] if (h == 0 and qb == 0) else []), inc=(c_fin, 1))
            potpool.release(si["poi"], c_fin)
            opool.release(si["oui"], c_fin)
            if (h, qb) == (1, NQB - 1):
                oc = c_osp[pair % 2]
                p.op("sync", lambda sy, pair=pair, otsl=otsl: sy.dma_start(out=ot_sc[pair], in_=otsl[:]),
                     waits=[c_fin.mark()], inc=(oc, 16))
                otpool.release(ps_["oti"], oc)

        def start_pair(pair):
            (qsl, qi), (ksl, ki), (vol, vi), ldm = pend.pop(pair)
            nm_done = norm_chain(pair, qsl, ksl, ldm)
            otsl, otw, oti, _ = otpool.acquire()
            pair_state[pair] = dict(qsl=qsl, qi=qi, ksl=ksl, ki=ki, vol=vol, vi=vi,
                                    nm_done=nm_done, otsl=otsl, otw=otw, oti=oti)

        start_pair(0)
        NIT = len(items)
        for idx in range(NIT):
            pair, h, qb = items[idx]
            if (h, qb) == (0, 0) and pair > 0:
                start_pair(pair)
            stage_S(idx)
            if (h, qb) == (1, NQB - 1):
                pair_ex_end[pair] = c_ex.val
            if (h, qb) == (0, 1) and pair + 1 < NPAIR:
                pend[pair + 1] = load_pair(pair + 1)
            if idx >= 1:
                stage_PV(idx - 1)
            if idx >= 2:
                stage_NORM(idx - 2)
        stage_PV(NIT - 1)
        stage_NORM(NIT - 2)
        stage_NORM(NIT - 1)

        p.barrier("end_p2", extra_waits=[c_osp[0].mark(), c_osp[1].mark(),
                                         c_alq[0].mark(), c_alq[1].mark(),
                                         c_alk[0].mark(), c_alk[1].mark(),
                                         c_alv[0].mark(), c_alv[1].mark(),
                                         c_dqm[0].mark(), c_dqm[1].mark()])
        S2.close()

        S3 = ES.enter_context(contextlib.ExitStack())
        sb3 = lambda *a: S3.enter_context(nc.sbuf_tensor(*a))
        ps3 = lambda *a: S3.enter_context(nc.psum_tensor(*a))
        wproj_bf = sb3("wproj_bf", [128, 8, C], BF16)
        otc_ = [sb3(f"otc{i}", [128, 8, 128], BF16) for i in range(2)]
        xti_ = [sb3(f"xti{i}", [128, C], F32) for i in range(2)]
        x1t_ = [sb3(f"x1t{i}", [128, C], F32) for i in range(2)]
        xTf_ = [sb3(f"xTf{i}", [128, C], F32) for i in range(2)]
        ln1b_ = [sb3(f"ln1b{i}", [128, 8, 128], BF16) for i in range(2)]
        xn1_ = [sb3(f"xn1_{i}", [128, C], F32) for i in range(2)]
        gbuf = sb3("gbuf", [128, 10], F32)
        gb2 = sb3("gb2", [128, 10], F32)
        gwk = sb3("gwk", [128, 8], F32)
        gwk2 = sb3("gwk2", [128, 8], F32)
        pproj = [ps3(f"pproj{i}", [128, C], F32) for i in range(2)]
        ptr3 = ps3("ptr3", [128, C], F32)
        plog = ps3("plog", [128, 8], F32)
        pcwt = ps3("pcwt", [8, 128], F32)

        c_wpl = p.ctr("wpl")
        c_otl = [p.ctr("otl0"), p.ctr("otl1")]
        c_xtl = [p.ctr("xtl0"), p.ctr("xtl1")]
        c_sx1 = [p.ctr("sx10"), p.ctr("sx11")]
        c_sln = [p.ctr("sln0"), p.ctr("sln1")]
        c_scw = p.ctr("scw")
        c_p3mm = p.ctr("p3mm")
        c_p3v = p.ctr("p3v")
        c_p3a = p.ctr("p3a")
        c_p3tr = p.ctr("p3tr")

        wpv = wproj_in.rearrange("(cc p) f -> cc p f", p=128)
        for cc in range(8):
            p.op("gpsimd", lambda g, cc=cc: g.dma_start(out=wproj_bf[:, cc, :], in_=wpv[cc]),
                 inc=(c_wpl, 16))
        wp_done = c_wpl.mark()

        otcpool = Pool(otc_, c_otl)
        xtipool = Pool(xti_, c_xtl)
        x1pool = Pool(x1t_)
        xTfpool = Pool(xTf_)
        ln1pool = Pool(ln1b_)
        xn1pool = Pool(xn1_)
        projpool = Pool(pproj)
        x1store = [None, None]
        lnstore = [None, None]
        outv = out.rearrange("(n p) c -> n p c", p=128)
        x1v = x1_sc.rearrange("(n p) c -> n p c", p=128)
        ptr3_free = None
        plog_free = None
        pcwt_free = None

        for ts in range(NTS):
            otcl, ow, oi, occ = otcpool.acquire()
            for pc in range(8):
                p.op("sync", lambda sy, ts=ts, pc=pc, otcl=otcl: sy.dma_start(
                    out=otcl[:, pc, :], in_=ot_sc[pc][:, ts * 128:(ts + 1) * 128]),
                    waits=(ow if pc == 0 else []), inc=(occ, 16))
            otm = occ.mark()
            xtil, xw, xi, xcc = xtipool.acquire()
            p.op("sync", lambda sy, ts=ts, xtil=xtil: sy.dma_start(out=xtil[:], in_=xview[ts]),
                 waits=xw, inc=(xcc, 16))
            xtm = xcc.mark()
            ppl, ppw, ppi, _ = projpool.acquire()
            for half in range(2):
                for pc in range(8):
                    w = []
                    if pc == 0 and half == 0:
                        w = [otm, wp_done] + ppw
                    p.op("tensor", lambda t, pc=pc, half=half, ppl=ppl, otcl=otcl: t.matmul(
                        ppl[:, half * 512:(half + 1) * 512], otcl[:, pc, :],
                        wproj_bf[:, pc, half * 512:(half + 1) * 512],
                        start=(pc == 0), stop=False),
                        waits=w)
                p.op("tensor", lambda t, half=half, ppl=ppl: t.matmul(
                    ppl[:, half * 512:(half + 1) * 512], ones_b[0:1, :],
                    bprow[0:1, half * 512:(half + 1) * 512],
                    start=False, stop=True),
                    inc=(c_p3mm, 1))
            mmd = c_p3mm.mark()
            otcpool.release(oi, c_p3mm)
            x1l, x1w, x1i, _ = x1pool.acquire()
            p.op("vector", lambda v, x1l=x1l, ppl=ppl, xtil=xtil: v.scalar_tensor_tensor(
                x1l[:], ppl[:], 1.0, xtil[:], op0=ALU.mult, op1=ALU.add),
                waits=[mmd, xtm] + x1w, inc=(c_p3v, 1))
            xtipool.release(xi, c_p3v)
            projpool.release(ppi, c_p3v)
            x1_done = c_p3v.mark()
            p.op("scalar", lambda s, x1l=x1l: s.activation(junk[:], x1l[:], AF.Identity, bias=0.0, scale=1.0, accum_out=gbuf[:, 8:9]),
                 waits=[x1_done, c_p3v.mark()], inc=(c_p3a, 1))
            p.op("scalar", lambda s, x1l=x1l: s.activation(junk[:], x1l[:], AF.Square, bias=0.0, scale=1.0, accum_out=gbuf[:, 9:10]),
                 waits=[c_p3a.mark()], inc=(c_p3a, 1))
            p.op("vector", lambda v: v.tensor_scalar(gb2[:, 0:1], gbuf[:, 8:9], 1.0 / C, None, op0=ALU.mult),
                 waits=[c_p3a.mark()], inc=(c_p3v, 1))
            p.op("vector", lambda v: v.tensor_tensor(gb2[:, 1:2], gb2[:, 0:1], gb2[:, 0:1], op=ALU.mult),
                 waits=[c_p3v.mark()], inc=(c_p3v, 1))
            p.op("vector", lambda v: v.scalar_tensor_tensor(gb2[:, 1:2], gbuf[:, 9:10], 1.0 / C, gb2[:, 1:2], op0=ALU.mult, op1=ALU.subtract),
                 waits=[c_p3v.mark()], inc=(c_p3v, 1))
            p.op("scalar", lambda s: s.activation(gb2[:, 1:2], gb2[:, 1:2], AF.Ln, bias=epsc[:], scale=1.0),
                 waits=[c_p3v.mark()], inc=(c_p3a, 1))
            p.op("scalar", lambda s: s.activation(gb2[:, 2:3], gb2[:, 1:2], AF.Exp, bias=0.0, scale=-0.5),
                 waits=[c_p3a.mark()], inc=(c_p3a, 1))
            xn1l, xnw, xni, _ = xn1pool.acquire()
            p.op("vector", lambda v, x1l=x1l, xn1l=xn1l: v.tensor_scalar(
                xn1l[:], x1l[:], gb2[:, 0:1], gb2[:, 2:3], op0=ALU.subtract, op1=ALU.mult),
                waits=[c_p3a.mark()] + xnw, inc=(c_p3v, 1))
            xn1_done = c_p3v.mark()
            sxc = c_sx1[ts % 2]
            p.op("sync", lambda sy, ts=ts, x1l=x1l: sy.dma_start(out=x1v[ts], in_=x1l[:]),
                 waits=[xn1_done], inc=(sxc, 16))
            x1pool.release(x1i, sxc)
            for cc in range(8):
                w = [xn1_done]
                if cc == 0 and ptr3_free is not None:
                    w.append(ptr3_free)
                p.op("tensor", lambda t, cc=cc, xn1l=xn1l: t.transpose(
                    ptr3[:, cc * 128:(cc + 1) * 128], xn1l[:, cc * 128:(cc + 1) * 128], idf[:]),
                    waits=w, inc=(c_p3tr, 1))
            trd = c_p3tr.mark()
            xn1pool.release(xni, c_p3tr)
            xTfl, xfw, xfi, _ = xTfpool.acquire()
            p.op("vector", lambda v, xTfl=xTfl: v.tensor_copy(xTfl[:], ptr3[:]),
                 waits=[trd] + xfw, inc=(c_p3v, 1))
            ln1l, lnw, lni, _ = ln1pool.acquire()
            p.op("vector", lambda v, ln1l=ln1l: v.tensor_copy(
                ln1l[:], ptr3[:].rearrange("p (cc t) -> p cc t", t=128)),
                waits=[trd] + lnw, inc=(c_p3v, 1))
            ptr3_free = c_p3v.mark()
            slc = c_sln[ts % 2]
            p.op("scalar", lambda s, ts=ts, ln1l=ln1l: s.dma_start(
                out=lnx_sc[:, :, ts * 128:(ts + 1) * 128], in_=ln1l[:]),
                waits=[c_p3v.mark()], inc=(slc, 16))
            ln1pool.release(lni, slc)
            for cc in range(8):
                w = [c_p3v.mark()]
                if cc == 0 and plog_free is not None:
                    w.append(plog_free)
                p.op("tensor", lambda t, cc=cc, xTfl=xTfl: t.matmul(
                    plog[:], xTfl[:, cc * 128:(cc + 1) * 128], wgT[:, cc, :],
                    start=(cc == 0), stop=(cc == 7)),
                    waits=w, inc=(c_p3mm, 1) if cc == 7 else None)
            xTfpool.release(xfi, c_p3mm)
            gmm = c_p3mm.mark()
            p.op("vector", lambda v: v.reduce_max(gbuf[:, 8:9], plog[:], axis=AX.X, negate=True),
                 waits=[gmm, c_p3v.mark()], inc=(c_p3v, 1))
            p.op("scalar", lambda s: s.activation(gbuf[:, 0:8], plog[:], AF.Exp, bias=gbuf[:, 8:9], scale=1.0, accum_out=gbuf[:, 9:10]),
                 waits=[c_p3v.mark()], inc=(c_p3a, 1))
            plog_free = c_p3a.mark()
            p.op("vector", lambda v: v.reciprocal(gb2[:, 3:4], gbuf[:, 9:10]),
                 waits=[c_p3a.mark()], inc=(c_p3v, 1))
            p.op("vector", lambda v: v.reduce_max(gb2[:, 4:5], gbuf[:, 0:8], axis=AX.X),
                 waits=[c_p3a.mark()], inc=(c_p3v, 1))
            p.op("vector", lambda v: v.tensor_scalar(gwk[:], gbuf[:, 0:8], gb2[:, 4:5], None, op0=ALU.is_ge),
                 waits=[c_p3v.mark()], inc=(c_p3v, 1))
            p.op("vector", lambda v: v.scalar_tensor_tensor(gwk2[:], gwk[:], -1e30, gbuf[:, 0:8], op0=ALU.mult, op1=ALU.add),
                 waits=[c_p3v.mark()], inc=(c_p3v, 1))
            p.op("vector", lambda v: v.reduce_max(gb2[:, 5:6], gwk2[:], axis=AX.X),
                 waits=[c_p3v.mark()], inc=(c_p3v, 1))
            p.op("vector", lambda v: v.tensor_scalar(gwk[:], gbuf[:, 0:8], gb2[:, 5:6], None, op0=ALU.is_ge),
                 waits=[c_p3v.mark()], inc=(c_p3v, 1))
            p.op("vector", lambda v: v.tensor_tensor(gwk2[:], gwk[:], gbuf[:, 0:8], op=ALU.mult),
                 waits=[c_p3v.mark()], inc=(c_p3v, 1))
            p.op("vector", lambda v, ts=ts: v.tensor_scalar(cw_all[:, ts, :], gwk2[:], gb2[:, 3:4], None, op0=ALU.mult),
                 waits=[c_p3v.mark()], inc=(c_p3v, 1))
            cw_done = c_p3v.mark()
            if debug:
                p.op("scalar", lambda s, ts=ts: s.dma_start(out=cw_sc[:, ts, :], in_=cw_all[:, ts, :]),
                     waits=[cw_done], inc=(c_scw, 16))
            w = [cw_done]
            if pcwt_free is not None:
                w.append(pcwt_free)
            p.op("tensor", lambda t, ts=ts: t.transpose(pcwt[:], cw_all[:, ts, :], idf[:]),
                 waits=w, inc=(c_p3tr, 1))
            p.op("vector", lambda v, ts=ts: v.tensor_copy(cwT[:, ts * 128:(ts + 1) * 128], pcwt[:]),
                 waits=[c_p3tr.mark()], inc=(c_p3v, 1))
            pcwt_free = c_p3v.mark()

        p.barrier("end_p3", extra_waits=[c_sx1[0].mark(), c_sx1[1].mark(),
                                         c_sln[0].mark(), c_sln[1].mark(), c_scw.mark(),
                                         c_otl[0].mark(), c_otl[1].mark(),
                                         c_xtl[0].mark(), c_xtl[1].mark(), c_wpl.mark()])
        S3.close()

        S5 = ES.enter_context(contextlib.ExitStack())
        sb5 = lambda *a: S5.enter_context(nc.sbuf_tensor(*a))
        ps5 = lambda *a: S5.enter_context(nc.psum_tensor(*a))
        lnxg = sb5("lnxg", [128, 8, GT], BF16)
        x1g_ = [sb5(f"x1g{i}", [128, C], F32) for i in range(2)]
        w1s_ = [sb5(f"w1s{i}", [128, 8, 512], BF16) for i in range(2)]
        w2h_ = [sb5(f"w2h{i}", [128, 16, C], BF16) for i in range(2)]
        ht_ = [sb5(f"ht{i}", [128, 16, GT], BF16) for i in range(1)]
        y_acc = sb5("y_acc", [128, 8, C], F32)
        php = [ps5(f"php{i}", [128, GT], F32) for i in range(2)]
        pyp = [ps5(f"pyp{i}", [128, C], F32) for i in range(2)]

        c_w1 = [p.ctr(f"w1_{i}") for i in range(2)]
        c_w2 = [p.ctr(f"w2_{i}") for i in range(2)]
        c_lnl = p.ctr("lnx_ld")
        c_x1g = [p.ctr("x1g0"), p.ctr("x1g1")]
        c_ffm = p.ctr("ffm")
        c_gel = p.ctr("gel")
        c_yev = p.ctr("yev")
        c_outs = [p.ctr(f"outs{i}") for i in range(8)]

        w1pool = Pool(w1s_, c_w1)
        w2pool = Pool(w2h_, c_w2)
        htpool = Pool(ht_)
        hpool = Pool(php)
        ypool = Pool(pyp)
        x1gpool = Pool(x1g_, c_x1g)
        ylast = [None] * 8
        outlast = [None] * 8
        lnxg_free = None

        for g in range(NG):
            w = [lnxg_free] if lnxg_free is not None else []
            p.op("sync", lambda sy, g=g: sy.dma_start(
                out=lnxg[:], in_=lnx_sc[:, :, g * GT:(g + 1) * GT]),
                waits=w, inc=(c_lnl, 16))
            lnx_ld = c_lnl.mark()
            for ts in range(8):
                x1l, xw, xi, xcc = x1gpool.acquire()
                p.op("sync", lambda sy, g=g, ts=ts, x1l=x1l: sy.dma_start(out=x1l[:], in_=x1v[g * 8 + ts]),
                     waits=xw, inc=(xcc, 16))
                x1m = xcc.mark()
                yp, yw, yi, _ = ypool.acquire()
                tok = (g * 8 + ts) * 128
                for half in range(2):
                    p.op("tensor", lambda t, half=half, yp=yp, tok=tok: t.matmul(
                        yp[:, half * 512:(half + 1) * 512], cwT[:, tok:tok + 128],
                        b2row[:, half * 512:(half + 1) * 512], start=True, stop=True),
                        waits=(yw if half == 0 else []), inc=(c_ffm, 1) if half == 1 else None)
                w = [c_ffm.mark(), x1m]
                if outlast[ts] is not None:
                    w.append(outlast[ts])
                if ylast[ts] is not None:
                    w.append(ylast[ts])
                p.op("vector", lambda v, ts=ts, yp=yp, x1l=x1l: v.tensor_tensor(
                    y_acc[:, ts, :], yp[:], x1l[:], op=ALU.add),
                    waits=w, inc=(c_yev, 1))
                ylast[ts] = c_yev.mark()
                ypool.release(yi, c_yev)
                x1gpool.release(xi, c_yev)
            w1_jobs = [(e_, h_ * 4 + si_) for e_ in range(E) for h_ in range(2) for si_ in range(4)]
            w1_fifo = []
            w1_cur = [0]

            def prefetch_w1():
                if w1_cur[0] >= len(w1_jobs):
                    return
                e_, s_ = w1_jobs[w1_cur[0]]
                w1_cur[0] += 1
                w1l, w1w, w1i, w1c = w1pool.acquire()
                w1view = w1_in[e_].rearrange("(cg p) (s f) -> s p cg f", p=128, f=512)
                p.op("gpsimd", lambda gq, w1l=w1l, w1view=w1view, s_=s_: gq.dma_start(
                    out=w1l[:], in_=w1view[s_]),
                    waits=w1w, inc=(w1c, 16))
                w1_fifo.append((w1l, w1c.mark(), w1i))

            prefetch_w1()
            prefetch_w1()
            for e in range(E):
                for half in range(2):
                    htl, htw, hti, _ = htpool.acquire()
                    w2l, w2w, w2i, w2c = w2pool.acquire()
                    w2view = w2_in[e].rearrange("(hh hc p) d -> hh p hc d", hc=16, p=128)
                    p.op("gpsimd", lambda gq, w2l=w2l, w2view=w2view, half=half: gq.dma_start(
                        out=w2l[:], in_=w2view[half]),
                        waits=w2w, inc=(w2c, 16))
                    w2m = w2c.mark()
                    for si in range(4):
                        s = half * 4 + si
                        w1l, w1m, w1i = w1_fifo.pop(0)
                        for j in range(4):
                            hf = s * 4 + j
                            hfh = si * 4 + j
                            hp, hw, hi, _ = hpool.acquire()
                            for cbk in range(8):
                                for th in range(2):
                                    w = []
                                    if cbk == 0 and th == 0:
                                        w = hw + [w1m, lnx_ld]
                                    p.op("tensor", lambda t, hp=hp, w1l=w1l, cbk=cbk, j=j, th=th: t.matmul(
                                        hp[:, th * 512:(th + 1) * 512],
                                        w1l[:, cbk, j * 128:(j + 1) * 128],
                                        lnxg[:, cbk, th * 512:(th + 1) * 512],
                                        start=(cbk == 0), stop=(cbk == 7)),
                                        waits=w,
                                        inc=(c_ffm, 1) if (cbk == 7 and th == 1) else None)
                            w = [c_ffm.mark()]
                            if hfh == 0:
                                w += htw
                            p.op("scalar", lambda sc, htl=htl, hfh=hfh, hp=hp, hf=hf, e=e: sc.activation(
                                htl[:, hfh, :], hp[:], AF.Gelu, bias=b1T[:, hf, e:e + 1], scale=1.0),
                                waits=w, inc=(c_gel, 1))
                            hpool.release(hi, c_gel)
                        w1pool.release(w1i, c_ffm)
                        prefetch_w1()
                    a_done = c_gel.mark()
                    if debug and g == 0 and e == 0 and half == 0:
                        p.op("sync", lambda sy, htl=htl: sy.dma_start(out=ht_sc[:], in_=htl[:]),
                             waits=[a_done], inc=(c_lnl, 16))
                    for ts in range(8):
                        yp, yw, yi, _ = ypool.acquire()
                        for hfh in range(16):
                            for h2 in range(2):
                                w = []
                                if hfh == 0 and h2 == 0:
                                    w = yw + [a_done, w2m]
                                p.op("tensor", lambda t, yp=yp, htl=htl, w2l=w2l, hfh=hfh, h2=h2, ts=ts: t.matmul(
                                    yp[:, h2 * 512:(h2 + 1) * 512],
                                    htl[:, hfh, ts * 128:(ts + 1) * 128],
                                    w2l[:, hfh, h2 * 512:(h2 + 1) * 512],
                                    start=(hfh == 0), stop=(hfh == 15)),
                                    waits=w,
                                    inc=(c_ffm, 1) if (hfh == 15 and h2 == 1) else None)
                        mmd = c_ffm.mark()
                        p.op("vector", lambda v, yp=yp, g=g, ts=ts, e=e: v.scalar_tensor_tensor(
                            y_acc[:, ts, :], yp[:], cw_all[:, g * 8 + ts, e:e + 1],
                            y_acc[:, ts, :], op0=ALU.mult, op1=ALU.add),
                            waits=[mmd, ylast[ts]], inc=(c_yev, 1))
                        ylast[ts] = c_yev.mark()
                        ypool.release(yi, c_yev)
                        if debug and g == 0 and e == 0 and half == 1:
                            p.op("sync", lambda sy, ts=ts: sy.dma_start(
                                out=y0_sc[:, ts, :], in_=y_acc[:, ts, :]),
                                waits=[ylast[ts]], inc=(c_outs[ts], 16))
                            ylast[ts] = (c_yev, c_yev.val)
                            outlast[ts] = c_outs[ts].mark() if outlast[ts] is None else outlast[ts]
                            ylast_dump = c_outs[ts].mark()
                            ylast[ts] = ylast_dump
                        if e == E - 1 and half == 1:
                            p.op("sync", lambda sy, g=g, ts=ts: sy.dma_start(
                                out=outv[g * 8 + ts], in_=y_acc[:, ts, :]),
                                waits=[ylast[ts]], inc=(c_outs[ts], 16))
                            outlast[ts] = c_outs[ts].mark()
                    htpool.release(hti, c_ffm)
                    w2pool.release(w2i, c_ffm)
            lnxg_free = c_ffm.mark()

        for ts in range(8):
            p.wait("sync", (c_outs[ts], c_outs[ts].val))
        with nc.Block() as block:
            p.build(block)
    return nc


_IDF = np.eye(128, dtype=np.float32)
_IDB = np.eye(128).astype(ml_dtypes.bfloat16)


def kernel(L0_attn_x, L0_attn_sin, L0_attn_cos, L0_attn_w_qkv, L0_attn_b_qkv,
           L0_attn_w_proj, L0_attn_b_proj, L0_ffn_w_gate, L0_ffn_W1, L0_ffn_b1,
           L0_ffn_W2, L0_ffn_b2):
    x = np.ascontiguousarray(np.asarray(L0_attn_x, dtype=np.float32))
    common = {
        "sin": np.ascontiguousarray(np.asarray(L0_attn_sin, dtype=np.float32)),
        "cos": np.ascontiguousarray(np.asarray(L0_attn_cos, dtype=np.float32)),
        "wqkv": np.ascontiguousarray(np.asarray(L0_attn_w_qkv, dtype=np.float32)),
        "bqkv": np.ascontiguousarray(np.asarray(L0_attn_b_qkv, dtype=np.float32)),
        "wproj": np.ascontiguousarray(np.asarray(L0_attn_w_proj, dtype=np.float32)),
        "bproj": np.ascontiguousarray(np.asarray(L0_attn_b_proj, dtype=np.float32)),
        "wg": np.ascontiguousarray(np.asarray(L0_ffn_w_gate, dtype=np.float32)),
        "w1": np.ascontiguousarray(np.asarray(L0_ffn_W1, dtype=np.float32)),
        "b1": np.ascontiguousarray(np.asarray(L0_ffn_b1, dtype=np.float32)),
        "w2": np.ascontiguousarray(np.asarray(L0_ffn_W2, dtype=np.float32)),
        "b2": np.ascontiguousarray(np.asarray(L0_ffn_b2, dtype=np.float32)),
        "idf": _IDF,
        "idb": _IDB,
    }
    nc = build_nc(debug=False)
    in_maps = [dict(common, x=np.ascontiguousarray(x[b])) for b in range(8)]
    res = run_bass_kernel_spmd(nc, in_maps, core_ids=list(range(8)))
    return np.stack([np.asarray(res.results[b]["out"]) for b in range(8)], axis=0)


# version 03
# speedup vs baseline: 1.1009x; 1.0627x over previous
import contextlib
import numpy as np
import ml_dtypes
import concourse.bass as bass
import concourse.mybir as mybir
import concourse.bass_utils as _bu
from concourse.bass_utils import run_bass_kernel_spmd

if not getattr(_bu, "_ldwopt_patched", False):
    _orig_run_command = _bu.run_command

    def _run_command_ldwopt(argv, **kwargs):
        argv = [a.replace("--enable-ldw-opt=false", "--enable-ldw-opt=true")
                if isinstance(a, str) else a for a in argv]
        return _orig_run_command(argv, **kwargs)

    _bu.run_command = _run_command_ldwopt
    _bu._ldwopt_patched = True

F32 = mybir.dt.float32
BF16 = mybir.dt.bfloat16
AF = mybir.ActivationFunctionType
ALU = mybir.AluOpType
AX = mybir.AxisListType

T = 4096
C = 1024
H = 16
DH = 64
NPAIR = 8
NROPE = 1369
PREFIX = T - NROPE
E = 8
HID = 4096
NG = 4
GT = T // NG
NTS = T // 128
QB = 512
NQB = T // QB
KT = T // 128

ENGINES = ("tensor", "vector", "scalar", "gpsimd", "sync")


class Ctr:
    __slots__ = ("name", "sem", "val")

    def __init__(self, name, sem):
        self.name = name
        self.sem = sem
        self.val = 0

    def mark(self):
        return (self, self.val)


class Prog:
    def __init__(self, nc, stack):
        self.nc = nc
        self.stack = stack
        self.stream = {e: [] for e in ENGINES}

    def ctr(self, name):
        sem = self.stack.enter_context(self.nc.semaphore(name))
        return Ctr(name, sem)

    def op(self, eng, fn, waits=(), inc=None):
        wl = []
        for w in waits:
            if w is None:
                continue
            c, v = w if isinstance(w, tuple) else (w, w.val)
            assert v <= c.val, f"wait on future value {c.name} {v} > {c.val}"
            if v > 0:
                wl.append((c, v))
        if inc is not None:
            c, n = inc
            c.val += n
        self.stream[eng].append((fn, tuple(wl), inc))

    def wait(self, eng, *waits):
        self.op(eng, None, waits=waits)

    def barrier(self, name, extra_waits=()):
        b = self.ctr(name)
        for e in ENGINES:
            w = tuple(extra_waits) if e == "sync" else ()
            self.op(e, (lambda eng: eng.drain()), waits=w, inc=(b, 1))
        for e in ENGINES:
            self.wait(e, (b, len(ENGINES)))
        return b

    def build(self, block):
        for e in ENGINES:
            self._replay(getattr(block, e), self.stream[e])

    def _replay(self, deco, ops):
        @deco
        def _(eng):
            for fn, waits, inc in ops:
                for c, v in waits:
                    eng.wait_ge(c.sem, v)
                ins = fn(eng) if fn is not None else None
                if inc is not None:
                    c, n = inc
                    if ins is None:
                        ins = eng.nop()
                    ins.then_inc(c.sem, n)


class Pool:

    def __init__(self, slots, ctrs=None):
        self.slots = list(slots)
        self.ctrs = ctrs
        self.markers = [None] * len(self.slots)
        self.i = 0

    def acquire(self):
        s = self.i % len(self.slots)
        self.i += 1
        w = self.markers[s]
        ctr = self.ctrs[s] if self.ctrs is not None else None
        return self.slots[s], ([w] if w is not None else []), s, ctr

    def release(self, s, ctr):
        self.markers[s] = ctr.mark()


def build_nc(debug=False):
    nc = bass.Bass("TRN2", target_bir_lowering=False)
    dbg_kind = "ExternalOutput" if debug else "Internal"

    x_in = nc.dram_tensor("x", [T, C], F32, kind="ExternalInput").ap()
    sin_in = nc.dram_tensor("sin", [NROPE, DH], F32, kind="ExternalInput").ap()
    cos_in = nc.dram_tensor("cos", [NROPE, DH], F32, kind="ExternalInput").ap()
    wqkv_in = nc.dram_tensor("wqkv", [C, 3 * C], F32, kind="ExternalInput").ap()
    bqkv_in = nc.dram_tensor("bqkv", [3 * C], F32, kind="ExternalInput").ap()
    wproj_in = nc.dram_tensor("wproj", [C, C], F32, kind="ExternalInput").ap()
    bproj_in = nc.dram_tensor("bproj", [C], F32, kind="ExternalInput").ap()
    wg_in = nc.dram_tensor("wg", [E, C], F32, kind="ExternalInput").ap()
    w1_in = nc.dram_tensor("w1", [E, C, HID], F32, kind="ExternalInput").ap()
    b1_in = nc.dram_tensor("b1", [E, HID], F32, kind="ExternalInput").ap()
    w2_in = nc.dram_tensor("w2", [E, HID, C], F32, kind="ExternalInput").ap()
    b2_in = nc.dram_tensor("b2", [E, C], F32, kind="ExternalInput").ap()
    idf_in = nc.dram_tensor("idf", [128, 128], F32, kind="ExternalInput").ap()
    idb_in = nc.dram_tensor("idb", [128, 128], BF16, kind="ExternalInput").ap()
    out = nc.dram_tensor("out", [T, C], F32, kind="ExternalOutput").ap()

    qt_sc = nc.dram_tensor("qt_sc", [NPAIR, 128, T], BF16, kind=dbg_kind).ap()
    kt_sc = nc.dram_tensor("kt_sc", [NPAIR, 128, T], BF16, kind=dbg_kind).ap()
    von_sc = nc.dram_tensor("von_sc", [NPAIR, 128, 2 * KT * 65], BF16, kind=dbg_kind).ap()
    ot_sc = nc.dram_tensor("ot_sc", [NPAIR, 128, T], BF16, kind=dbg_kind).ap()
    x1_sc = nc.dram_tensor("x1_sc", [T, C], F32, kind=dbg_kind).ap()
    lnx_sc = nc.dram_tensor("lnx_sc", [128, 8, T], BF16, kind=dbg_kind).ap()
    cw_sc = nc.dram_tensor("cw_sc", [128, NTS, E], F32, kind=dbg_kind).ap()
    ht_sc = nc.dram_tensor("ht_sc", [128, 16, GT], BF16, kind=dbg_kind).ap()
    y0_sc = nc.dram_tensor("y0_sc", [128, 8, C], F32, kind=dbg_kind).ap()

    ES = contextlib.ExitStack()
    with ES:
        sb = lambda *a: ES.enter_context(nc.sbuf_tensor(*a))
        psum = lambda *a: ES.enter_context(nc.psum_tensor(*a))
        p = Prog(nc, ES)

        idf = sb("idf_s", [128, 128], F32)
        idb = sb("idb_s", [128, 128], BF16)
        ones_f = sb("ones_f", [1, 128], F32)
        ones_b = sb("ones_b", [1, 128], BF16)
        maskh = sb("maskh", [128, 2, 128], BF16)
        bq_sb = sb("bq_sb", [128, 24], F32)
        bprow = sb("bprow", [1, C], BF16)
        b2row = sb("b2row", [E, C], F32)
        b1T = sb("b1T", [128, 32, E], F32)
        wgT = sb("wgT", [128, 8, E], F32)
        sinF = sb("sinF", [128, NROPE], BF16)
        cosF = sb("cosF", [128, NROPE], BF16)
        cw_all = sb("cw_all", [128, NTS, E], F32)
        cwT = sb("cwT", [8, T], F32)
        epsc = sb("epsc", [128, 1], F32)
        junk = sb("junk", [128, 1024], F32)

        c_ic = p.ctr("ic")
        c_icv = p.ctr("icv")
        c_ica = p.ctr("ica")

        SC = ES.enter_context(contextlib.ExitStack())
        sbc = lambda *a: SC.enter_context(nc.sbuf_tensor(*a))
        bq_stage = sbc("bq_stage", [24, 128], F32)
        b1_stage = sbc("b1_stage", [8, HID], F32)
        wg_stage = sbc("wg_stage", [8, C], F32)
        sin_stage = sbc("sin_stage", [128, 11, DH], F32)
        cos_stage = sbc("cos_stage", [128, 11, DH], F32)

        p.op("sync", lambda sy: sy.dma_start(out=idf[:], in_=idf_in), inc=(c_ic, 16))
        p.op("sync", lambda sy: sy.dma_start(out=idb[:], in_=idb_in), inc=(c_ic, 16))
        p.op("sync", lambda sy: sy.dma_start(out=bq_stage[:], in_=bqkv_in.rearrange("(a b) -> a b", b=128)[0:24]), inc=(c_ic, 16))
        p.op("sync", lambda sy: sy.dma_start(out=b2row[:], in_=b2_in), inc=(c_ic, 16))
        p.op("sync", lambda sy: sy.dma_start(out=b1_stage[:], in_=b1_in), inc=(c_ic, 16))
        p.op("sync", lambda sy: sy.dma_start(out=wg_stage[:], in_=wg_in), inc=(c_ic, 16))
        p.op("sync", lambda sy: sy.dma_start(out=sin_stage[0:128, 0:10, :], in_=sin_in[0:1280].rearrange("(n p) d -> p n d", p=128)), inc=(c_ic, 16))
        p.op("sync", lambda sy: sy.dma_start(out=sin_stage[0:89, 10, :], in_=sin_in[1280:NROPE, :]), inc=(c_ic, 16))
        p.op("sync", lambda sy: sy.dma_start(out=cos_stage[0:128, 0:10, :], in_=cos_in[0:1280].rearrange("(n p) d -> p n d", p=128)), inc=(c_ic, 16))
        p.op("sync", lambda sy: sy.dma_start(out=cos_stage[0:89, 10, :], in_=cos_in[1280:NROPE, :]), inc=(c_ic, 16))
        p.op("gpsimd", lambda g: g.dma_start(out=bprow[:], in_=bproj_in.rearrange("(a b) -> a b", a=1)), inc=(c_ic, 16))

        p.op("vector", lambda v: v.memset(ones_f[:], 1.0), inc=(c_icv, 1))
        p.op("vector", lambda v: v.memset(ones_b[:], 1.0), inc=(c_icv, 1))
        p.op("vector", lambda v: v.memset(maskh[:], 0.0), inc=(c_icv, 1))
        p.op("vector", lambda v: v.memset(maskh[0:64, 0, :], 1.0), waits=[c_icv.mark()], inc=(c_icv, 1))
        p.op("vector", lambda v: v.memset(maskh[64:128, 1, :], 1.0), waits=[c_icv.mark()], inc=(c_icv, 1))
        p.op("vector", lambda v: v.memset(epsc[:], 1e-5), inc=(c_icv, 1))

        pconst = SC.enter_context(nc.psum_tensor("pconst", [128, 512], F32))
        pconst2 = SC.enter_context(nc.psum_tensor("pconst2", [64, 1408], F32))
        ld_all = c_ic.mark()
        p.op("tensor", lambda t: t.transpose(pconst[:, 0:24], bq_stage[:], idf[0:24, 0:24]),
             waits=[ld_all], inc=(c_ica, 1))
        p.op("vector", lambda v: v.tensor_copy(bq_sb[:], pconst[:, 0:24]),
             waits=[c_ica.mark()], inc=(c_icv, 1))
        for hf in range(32):
            p.op("tensor", lambda t, hf=hf: t.transpose(pconst[:, 32 + hf * 8:40 + hf * 8], b1_stage[:, hf * 128:(hf + 1) * 128], idf[0:8, 0:8]),
                 waits=[ld_all, c_icv.mark()], inc=(c_ica, 1))
        p.op("vector", lambda v: v.tensor_copy(b1T[:], pconst[:, 32:32 + 256].rearrange("p (hf e) -> p hf e", e=8)),
             waits=[c_ica.mark()], inc=(c_icv, 1))
        for cc in range(8):
            p.op("tensor", lambda t, cc=cc: t.transpose(pconst[:, 320 + cc * 8:328 + cc * 8], wg_stage[:, cc * 128:(cc + 1) * 128], idf[0:8, 0:8]),
                 waits=[ld_all, c_icv.mark()], inc=(c_ica, 1))
        p.op("vector", lambda v: v.tensor_copy(wgT[:], pconst[:, 320:320 + 64].rearrange("p (cc e) -> p cc e", e=8)),
             waits=[c_ica.mark()], inc=(c_icv, 1))
        for stage, dstF, negate in ((sin_stage, sinF, True), (cos_stage, cosF, False)):
            for i in range(11):
                n = 128 if i < 10 else 89
                p.op("tensor", lambda t, i=i, n=n, stage=stage: t.transpose(
                    pconst2[:, i * 128:i * 128 + n], stage[0:n, i, :], idf[0:n, 0:n]),
                    waits=[ld_all, c_icv.mark()], inc=(c_ica, 1))
            p.op("vector", lambda v, dstF=dstF: v.tensor_copy(dstF[0:64, :], pconst2[:, 0:NROPE]),
                 waits=[c_ica.mark()], inc=(c_icv, 1))
            if negate:
                p.op("vector", lambda v, dstF=dstF: v.tensor_scalar(
                    dstF[0:32, :], dstF[0:32, :], -1.0, None, op0=ALU.mult),
                    waits=[c_icv.mark()], inc=(c_icv, 1))
            p.op("scalar", lambda s, dstF=dstF: s.dma_start(out=dstF[64:128, :], in_=dstF[0:64, :]),
                 waits=[c_icv.mark()], inc=(c_ic, 16))
        consts_done = [c_ic.mark(), c_icv.mark(), c_ica.mark()]
        SC.close()

        S01 = ES.enter_context(contextlib.ExitStack())
        sb1 = lambda *a: S01.enter_context(nc.sbuf_tensor(*a))
        ps1 = lambda *a: S01.enter_context(nc.psum_tensor(*a))
        xnT = sb1("xnT", [128, 8, T], BF16)
        wqkv_bf = sb1("wqkv_bf", [128, 8, 3 * C], BF16)
        xt = [sb1(f"xt{i}", [128, C], F32) for i in range(2)]
        xnb = [sb1(f"xn{i}", [128, C], BF16) for i in range(2)]
        stat = sb1("stat", [128, 8], F32)
        rstd2 = sb1("rstd2", [128, 2], F32)
        ptr = [ps1("ptr0", [128, 128], BF16), ps1("ptr1", [128, 128], BF16)]

        c_x = [p.ctr("x_ld0"), p.ctr("x_ld1")]
        c_wq = p.ctr("wq_ld")
        c_lnv = p.ctr("lnv")
        c_lna = p.ctr("lna")
        c_tr = p.ctr("tr")
        c_xnt = p.ctr("xnt")

        wv = wqkv_in.rearrange("(cc p) f -> cc p f", p=128)
        for cc in range(8):
            p.op("gpsimd", lambda g, cc=cc: g.dma_start(out=wqkv_bf[:, cc, :], in_=wv[cc]),
                 waits=(consts_done if cc == 0 else ()), inc=(c_wq, 16))
        wq_done = c_wq.mark()

        xview = x_in.rearrange("(n p) c -> n p c", p=128)
        xtile_free = [None, None]
        xn_free = [None, None]
        for ts in range(NTS):
            b = ts % 2
            w = [consts_done[0], consts_done[1], consts_done[2]] if ts < 2 else []
            if xtile_free[b] is not None:
                w.append(xtile_free[b])
            p.op("sync", lambda sy, ts=ts, b=b: sy.dma_start(out=xt[b][:], in_=xview[ts]),
                 waits=w, inc=(c_x[b], 16))
            xld = c_x[b].mark()
            o = 4 * b
            p.op("scalar", lambda s, b=b, o=o: s.activation(junk[:], xt[b][:], AF.Identity, bias=0.0, scale=1.0, accum_out=stat[:, o:o + 1]),
                 waits=[xld, c_lnv.mark()], inc=(c_lna, 1))
            p.op("scalar", lambda s, b=b, o=o: s.activation(junk[:], xt[b][:], AF.Square, bias=0.0, scale=1.0, accum_out=stat[:, o + 1:o + 2]),
                 waits=[c_lna.mark()], inc=(c_lna, 1))
            acc_done = c_lna.mark()
            p.op("vector", lambda v, o=o: v.tensor_scalar(stat[:, o + 2:o + 3], stat[:, o:o + 1], 1.0 / C, None, op0=ALU.mult),
                 waits=[acc_done], inc=(c_lnv, 1))
            p.op("vector", lambda v, o=o: v.tensor_tensor(stat[:, o + 3:o + 4], stat[:, o + 2:o + 3], stat[:, o + 2:o + 3], op=ALU.mult),
                 waits=[c_lnv.mark()], inc=(c_lnv, 1))
            p.op("vector", lambda v, o=o: v.scalar_tensor_tensor(stat[:, o + 3:o + 4], stat[:, o + 1:o + 2], 1.0 / C, stat[:, o + 3:o + 4], op0=ALU.mult, op1=ALU.subtract),
                 waits=[c_lnv.mark()], inc=(c_lnv, 1))
            p.op("scalar", lambda s, o=o: s.activation(stat[:, o + 3:o + 4], stat[:, o + 3:o + 4], AF.Ln, bias=epsc[:], scale=1.0),
                 waits=[c_lnv.mark()], inc=(c_lna, 1))
            p.op("scalar", lambda s, b=b, o=o: s.activation(rstd2[:, b:b + 1], stat[:, o + 3:o + 4], AF.Exp, bias=0.0, scale=-0.5),
                 waits=[c_lna.mark()], inc=(c_lna, 1))
            w2_ = [c_lna.mark()]
            if xn_free[b] is not None:
                w2_.append(xn_free[b])
            p.op("vector", lambda v, b=b, o=o: v.tensor_scalar(xnb[b][:], xt[b][:], stat[:, o + 2:o + 3], rstd2[:, b:b + 1], op0=ALU.subtract, op1=ALU.mult),
                 waits=w2_, inc=(c_lnv, 1))
            xtile_free[b] = c_lnv.mark()
            xn_done = c_lnv.mark()
            for cc in range(8):
                pt = ptr[cc % 2]
                tw = [xn_done]
                gidx = ts * 8 + cc
                if gidx >= 2:
                    tw.append((c_xnt, gidx - 1))
                p.op("tensor", lambda t, b=b, cc=cc, pt=pt: t.transpose(pt[:], xnb[b][:, cc * 128:(cc + 1) * 128], idb[:]),
                     waits=tw, inc=(c_tr, 1))
                p.op("vector", lambda v, ts=ts, cc=cc, pt=pt: v.tensor_copy(xnT[:, cc, ts * 128:(ts + 1) * 128], pt[:]),
                     waits=[c_tr.mark()], inc=(c_xnt, 1))
            xn_free[b] = c_xnt.mark()
        xnT_done = c_xnt.mark()

        qt_buf = sb1("qt_buf", [128, T], BF16)
        kt_buf = sb1("kt_buf", [128, T], BF16)
        vt_buf = sb1("vt_buf", [128, T], BF16)
        von_buf = sb1("von_buf", [128, 2, KT, 65], BF16)
        rot = sb1("rot", [128, NROPE], BF16)
        rtmp = sb1("rtmp", [128, NROPE], BF16)
        pqk = [ps1(f"pqk{i}", [128, 512], F32) for i in range(4)]
        pvt = [ps1(f"pvt{i}", [128, 128], BF16) for i in range(2)]

        c_p1mm = p.ctr("p1mm")
        c_p1ev = p.ctr("p1ev")
        c_rope = p.ctr("rope")
        c_spq = p.ctr("spq")
        c_spk = p.ctr("spk")
        c_spv = p.ctr("spv")
        c_vcp = p.ctr("vcp")

        qk_pool = Pool(pqk)
        vt_pool = Pool(pvt)

        def proj_chunk(fc, dst_buf, buf_free):
            for tb in range(8):
                pb, pw, ps_, _ = qk_pool.acquire()
                for cbk in range(8):
                    w = []
                    if cbk == 0:
                        w += pw
                        if tb == 0:
                            w += [xnT_done, wq_done]
                            if buf_free is not None:
                                w.append(buf_free)
                    p.op("tensor", lambda t, fc=fc, tb=tb, cbk=cbk, pb=pb: t.matmul(
                        pb[:], wqkv_bf[:, cbk, fc * 128:(fc + 1) * 128],
                        xnT[:, cbk, tb * 512:(tb + 1) * 512],
                        start=(cbk == 0), stop=(cbk == 7)),
                        waits=w, inc=(c_p1mm, 1) if cbk == 7 else None)
                p.op("scalar", lambda s, fc=fc, tb=tb, pb=pb, dst_buf=dst_buf: s.activation(
                    dst_buf[:, tb * 512:(tb + 1) * 512], pb[:], AF.Identity,
                    bias=bq_sb[:, fc:fc + 1], scale=1.0),
                    waits=[c_p1mm.mark()], inc=(c_p1ev, 1))
                qk_pool.release(ps_, c_p1ev)
            return c_p1ev.mark()

        def rope_inplace(dst_buf, ev_done):
            for (a, b_) in ((0, 32), (32, 0), (64, 96), (96, 64)):
                p.op("scalar", lambda s, a=a, b_=b_, dst_buf=dst_buf: s.dma_start(
                    out=rot[a:a + 32, :], in_=dst_buf[b_:b_ + 32, PREFIX:T]),
                    waits=[ev_done, c_rope.mark()], inc=(c_rope, 16))
            p.op("vector", lambda v: v.tensor_tensor(rtmp[:], rot[:], sinF[:], op=ALU.mult),
                 waits=[c_rope.mark()], inc=(c_rope, 1))
            p.op("vector", lambda v, dst_buf=dst_buf: v.tensor_tensor(rot[:], dst_buf[:, PREFIX:T], cosF[:], op=ALU.mult),
                 waits=[c_rope.mark()], inc=(c_rope, 1))
            p.op("vector", lambda v, dst_buf=dst_buf: v.tensor_tensor(dst_buf[:, PREFIX:T], rot[:], rtmp[:], op=ALU.add),
                 waits=[c_rope.mark()], inc=(c_rope, 1))
            return c_rope.mark()

        pq_free = pk_free = pv_free = pvon_free = None
        for pair in range(NPAIR):
            ev = proj_chunk(pair, qt_buf, pq_free)
            rp = rope_inplace(qt_buf, ev)
            p.op("scalar", lambda s, pair=pair: s.dma_start(out=qt_sc[pair], in_=qt_buf[:]),
                 waits=[rp], inc=(c_spq, 16))
            pq_free = c_spq.mark()
            ev = proj_chunk(8 + pair, kt_buf, pk_free)
            rp = rope_inplace(kt_buf, ev)
            p.op("scalar", lambda s, pair=pair: s.dma_start(out=kt_sc[pair], in_=kt_buf[:]),
                 waits=[rp], inc=(c_spk, 16))
            pk_free = c_spk.mark()
            ev = proj_chunk(16 + pair, vt_buf, pv_free)
            w = [pvon_free] if pvon_free is not None else []
            p.op("vector", lambda v: v.memset(von_buf[:, :, :, 64:65], 1.0),
                 waits=w + [c_vcp.mark()], inc=(c_vcp, 1))
            for kt in range(KT):
                pt, pw, ps_, _ = vt_pool.acquire()
                p.op("tensor", lambda t, kt=kt, pt=pt: t.transpose(
                    pt[:], vt_buf[:, kt * 128:(kt + 1) * 128], idb[:]),
                    waits=pw + ([ev] if kt == 0 else []), inc=(c_tr, 1))
                trm = c_tr.mark()
                p.op("vector", lambda v, kt=kt, pt=pt: v.tensor_copy(von_buf[:, 0, kt, 0:64], pt[:, 0:64]),
                     waits=[trm], inc=(c_vcp, 1))
                p.op("vector", lambda v, kt=kt, pt=pt: v.tensor_copy(von_buf[:, 1, kt, 0:64], pt[:, 64:128]),
                     waits=[trm], inc=(c_vcp, 1))
                vt_pool.release(ps_, c_vcp)
            pv_free = c_tr.mark()
            p.op("scalar", lambda s, pair=pair: s.dma_start(
                out=von_sc[pair], in_=von_buf[:].rearrange("p h kt f -> p (h kt f)")),
                waits=[c_vcp.mark()], inc=(c_spv, 16))
            pvon_free = c_spv.mark()

        p.barrier("end_p1", extra_waits=[c_spq.mark(), c_spk.mark(), c_spv.mark(),
                                         c_x[0].mark(), c_x[1].mark(), c_rope.mark(),
                                         c_ic.mark(), c_wq.mark()])
        S01.close()

        S2 = ES.enter_context(contextlib.ExitStack())
        sb2 = lambda *a: S2.enter_context(nc.sbuf_tensor(*a))
        ps2 = lambda *a: S2.enter_context(nc.psum_tensor(*a))
        qs_ = [sb2(f"qs{i}", [128, T], BF16) for i in range(2)]
        ks_ = [sb2(f"ks{i}", [128, T], BF16) for i in range(2)]
        vo_ = [sb2(f"vo{i}", [128, 2, KT, 65], BF16) for i in range(2)]
        strips = [sb2(f"strip{i}", [128, KT, QB], BF16) for i in range(2)]
        otu_ = [sb2(f"otu{i}", [65, QB], F32) for i in range(2)]
        dq_ = [sb2(f"dq{i}", [1, QB], F32) for i in range(2)]
        rq_ = [sb2(f"rq{i}", [1, QB], F32) for i in range(2)]
        otp_ = [sb2(f"otp{i}", [128, T], BF16) for i in range(2)]
        sqb = sb2("sqb", [128, T], BF16)
        nbuf = sb2("nbuf", [128, 16], F32)
        nmc = sb2("nmc", [128, 4], F32)
        s2p = [ps2(f"s2p{i}", [128, 2, 512], F32) for i in range(2)]
        potp = [ps2(f"potp{i}", [65, QB], F32) for i in range(3)]

        c_alq = [p.ctr("alq0"), p.ctr("alq1")]
        c_alk = [p.ctr("alk0"), p.ctr("alk1")]
        c_alv = [p.ctr("alv0"), p.ctr("alv1")]
        c_smm = p.ctr("smm")
        c_ex = p.ctr("ex")
        c_pv = p.ctr("pv")
        c_oev = p.ctr("oev")
        c_dqm = [p.ctr("dqm0"), p.ctr("dqm1")]
        c_rq = p.ctr("rqm")
        c_bc = p.ctr("bc")
        c_fin = p.ctr("fin")
        c_nrm = p.ctr("nrm")
        c_osp = [p.ctr("osp0"), p.ctr("osp1")]

        qpool = Pool(qs_, c_alq)
        kpool = Pool(ks_, c_alk)
        vpool = Pool(vo_, c_alv)
        spool = Pool(strips)
        opool = Pool(otu_)
        dqpool = Pool(dq_, c_dqm)
        rqpool = Pool(rq_)
        otpool = Pool(otp_)
        s2pool = Pool(s2p)
        potpool = Pool(potp)

        def load_pair(pair):
            qsl, qw, qi, qc = qpool.acquire()
            ksl, kw, ki, kc = kpool.acquire()
            vol, vw, vi, vc = vpool.acquire()
            p.op("sync", lambda sy, pair=pair, qsl=qsl: sy.dma_start(out=qsl[:], in_=qt_sc[pair]),
                 waits=qw, inc=(qc, 16))
            p.op("sync", lambda sy, pair=pair, ksl=ksl: sy.dma_start(out=ksl[:], in_=kt_sc[pair]),
                 waits=kw, inc=(kc, 16))
            p.op("sync", lambda sy, pair=pair, vol=vol: sy.dma_start(
                out=vol[:].rearrange("p h kt f -> p (h kt f)"), in_=von_sc[pair]),
                 waits=vw, inc=(vc, 16))
            return (qsl, qi), (ksl, ki), (vol, vi), [qc.mark(), kc.mark(), vc.mark()]

        pair_ex_end = {}
        pend = {0: load_pair(0)}
        pair_state = {}

        def norm_chain(pair, qsl, ksl, ldm):
            nmo = (pair % 2) * 2
            for side, buf in ((0, qsl), (1, ksl)):
                p.op("scalar", lambda s, buf=buf: s.activation(sqb[:], buf[:], AF.Square, bias=0.0, scale=1.0),
                     waits=ldm + [c_nrm.mark()], inc=(c_nrm, 1))
                sq_done = c_nrm.mark()
                for h in range(2):
                    ni = side * 2 + h
                    s2l, s2w, s2i, _ = s2pool.acquire()
                    for blk in range(8):
                        p.op("tensor", lambda t, h=h, blk=blk, s2l=s2l: t.matmul(
                            s2l[:, blk % 2, :], maskh[:, h, :], sqb[:, blk * 512:(blk + 1) * 512],
                            start=True, stop=True),
                            waits=[sq_done, c_nrm.mark()] + (s2w if blk == 0 else []), inc=(c_nrm, 1))
                        p.op("vector", lambda v, ni=ni, blk=blk, s2l=s2l: v.reduce_max(
                            nbuf[:, 8 + ni:9 + ni], s2l[:, blk % 2, :], axis=AX.X),
                            waits=[c_nrm.mark()], inc=(c_nrm, 1))
                        if blk == 0:
                            p.op("vector", lambda v, ni=ni: v.tensor_copy(nbuf[:, ni:ni + 1], nbuf[:, 8 + ni:9 + ni]),
                                 waits=[c_nrm.mark()], inc=(c_nrm, 1))
                        else:
                            p.op("vector", lambda v, ni=ni: v.tensor_tensor(
                                nbuf[:, ni:ni + 1], nbuf[:, ni:ni + 1], nbuf[:, 8 + ni:9 + ni], op=ALU.max),
                                waits=[c_nrm.mark()], inc=(c_nrm, 1))
                    s2pool.release(s2i, c_nrm)
            nmc_war = (c_ex, pair_ex_end[pair - 2]) if pair >= 2 else None
            for h in range(2):
                p.op("vector", lambda v, h=h: v.tensor_tensor(nbuf[:, 12 + h:13 + h], nbuf[:, h:h + 1], nbuf[:, 2 + h:3 + h], op=ALU.mult),
                     waits=[c_nrm.mark()], inc=(c_nrm, 1))
                p.op("scalar", lambda s, h=h: s.activation(nbuf[:, 12 + h:13 + h], nbuf[:, 12 + h:13 + h], AF.Ln, bias=epsc[:], scale=1.0),
                     waits=[c_nrm.mark()], inc=(c_nrm, 1))
                p.op("scalar", lambda s, h=h: s.activation(nbuf[:, 12 + h:13 + h], nbuf[:, 12 + h:13 + h], AF.Exp, bias=0.0, scale=0.5),
                     waits=[c_nrm.mark()], inc=(c_nrm, 1))
                p.op("vector", lambda v, h=h, nmo=nmo: v.tensor_scalar(nmc[:, nmo + h:nmo + h + 1], nbuf[:, 12 + h:13 + h], -0.127575, None, op0=ALU.mult),
                     waits=[c_nrm.mark(), nmc_war], inc=(c_nrm, 1))
            return c_nrm.mark()

        items = [(pair, h, qb) for pair in range(NPAIR) for h in range(2) for qb in range(NQB)]
        st_info = {}

        def stage_S(idx):
            pair, h, qb = items[idx]
            ps_ = pair_state[pair]
            qsl, ksl = ps_["qsl"], ps_["ksl"]
            nmo = (pair % 2) * 2
            stl, stw, sti, _ = spool.acquire()
            ex0 = c_ex.val
            for ktg in range(KT // 2):
                s2l, s2w, s2i, _ = s2pool.acquire()
                for sub in range(2):
                    kt = 2 * ktg + sub
                    w = list(s2w) if sub == 0 else []
                    if ktg == 0 and sub == 0:
                        w += [ps_["nm_done"]] + stw
                    p.op("tensor", lambda t, h=h, qb=qb, kt=kt, sub=sub, s2l=s2l, ksl=ksl, qsl=qsl: t.matmul(
                        s2l[:, sub, :], ksl[h * 64:(h + 1) * 64, kt * 128:(kt + 1) * 128],
                        qsl[h * 64:(h + 1) * 64, qb * QB:(qb + 1) * QB],
                        start=True, stop=True),
                        waits=w, inc=(c_smm, 1) if sub == 1 else None)
                p.op("scalar", lambda s, h=h, ktg=ktg, s2l=s2l, stl=stl, nmo=nmo: s.activation(
                    stl[:, 2 * ktg:2 * ktg + 2, :], s2l[:], AF.Exp,
                    bias=nmc[:, nmo + h:nmo + h + 1], scale=0.125),
                    waits=[c_smm.mark()], inc=(c_ex, 1))
                s2pool.release(s2i, c_ex)
            st_info[idx] = {"stl": stl, "sti": sti, "ex_done": c_ex.mark()}

        def stage_PV(idx):
            pair, h, qb = items[idx]
            ps_ = pair_state[pair]
            vol = ps_["vol"]
            si = st_info[idx]
            stl = si["stl"]
            pol, pow_, poi, _ = potpool.acquire()
            for kt in range(KT):
                w = []
                if kt == 0:
                    w = [si["ex_done"]] + pow_
                p.op("tensor", lambda t, h=h, kt=kt, pol=pol, vol=vol, stl=stl: t.matmul(
                    pol[:], vol[:, h, kt, :], stl[:, kt, :],
                    start=(kt == 0), stop=(kt == KT - 1)),
                    waits=w, inc=(c_pv, 1) if kt == KT - 1 else None)
            spool.release(si["sti"], c_pv)
            pv_done = c_pv.mark()
            if (h, qb) == (1, NQB - 1):
                qpool.release(ps_["qi"], c_pv)
                kpool.release(ps_["ki"], c_pv)
                vpool.release(ps_["vi"], c_pv)
            oul, ouw, oui, _ = opool.acquire()
            p.op("vector", lambda v, oul=oul, pol=pol: v.tensor_copy(oul[:], pol[:]),
                 waits=[pv_done] + ouw, inc=(c_oev, 1))
            ev_done = c_oev.mark()
            dql, dqw, dqi, dqc = dqpool.acquire()
            p.op("scalar", lambda s, dql=dql, oul=oul: s.dma_start(out=dql[:], in_=oul[64:65, :]),
                 waits=[ev_done] + dqw, inc=(dqc, 16))
            rql, rqw, rqi, _ = rqpool.acquire()
            p.op("vector", lambda v, rql=rql, dql=dql: v.reciprocal(rql[:], dql[:]),
                 waits=[dqc.mark()] + rqw, inc=(c_rq, 1))
            dqpool.release(dqi, c_rq)
            si.update(pol=pol, poi=poi, oul=oul, oui=oui, rql=rql, rqi=rqi,
                      ev_done=ev_done, rq_done=c_rq.mark())

        def stage_NORM(idx):
            pair, h, qb = items[idx]
            ps_ = pair_state[pair]
            si = st_info.pop(idx)
            pol, oul, rql = si["pol"], si["oul"], si["rql"]
            p.op("tensor", lambda t, pol=pol, rql=rql: t.matmul(
                pol[0:64, :], ones_f[0:1, 0:64], rql[:], start=True, stop=True),
                waits=[si["rq_done"], si["ev_done"]], inc=(c_bc, 1))
            rqpool.release(si["rqi"], c_bc)
            otsl = ps_["otsl"]
            p.op("vector", lambda v, h=h, qb=qb, otsl=otsl, oul=oul, pol=pol: v.tensor_tensor(
                otsl[h * 64:(h + 1) * 64, qb * QB:(qb + 1) * QB],
                oul[0:64, :], pol[0:64, :], op=ALU.mult),
                waits=[c_bc.mark()] + (ps_["otw"] if (h == 0 and qb == 0) else []), inc=(c_fin, 1))
            potpool.release(si["poi"], c_fin)
            opool.release(si["oui"], c_fin)
            if (h, qb) == (1, NQB - 1):
                oc = c_osp[pair % 2]
                p.op("sync", lambda sy, pair=pair, otsl=otsl: sy.dma_start(out=ot_sc[pair], in_=otsl[:]),
                     waits=[c_fin.mark()], inc=(oc, 16))
                otpool.release(ps_["oti"], oc)

        def start_pair(pair):
            (qsl, qi), (ksl, ki), (vol, vi), ldm = pend.pop(pair)
            nm_done = norm_chain(pair, qsl, ksl, ldm)
            otsl, otw, oti, _ = otpool.acquire()
            pair_state[pair] = dict(qsl=qsl, qi=qi, ksl=ksl, ki=ki, vol=vol, vi=vi,
                                    nm_done=nm_done, otsl=otsl, otw=otw, oti=oti)

        start_pair(0)
        NIT = len(items)
        for idx in range(NIT):
            pair, h, qb = items[idx]
            if (h, qb) == (0, 0) and pair > 0:
                start_pair(pair)
            stage_S(idx)
            if (h, qb) == (1, NQB - 1):
                pair_ex_end[pair] = c_ex.val
            if (h, qb) == (0, 1) and pair + 1 < NPAIR:
                pend[pair + 1] = load_pair(pair + 1)
            if idx >= 1:
                stage_PV(idx - 1)
            if idx >= 2:
                stage_NORM(idx - 2)
        stage_PV(NIT - 1)
        stage_NORM(NIT - 2)
        stage_NORM(NIT - 1)

        p.barrier("end_p2", extra_waits=[c_osp[0].mark(), c_osp[1].mark(),
                                         c_alq[0].mark(), c_alq[1].mark(),
                                         c_alk[0].mark(), c_alk[1].mark(),
                                         c_alv[0].mark(), c_alv[1].mark(),
                                         c_dqm[0].mark(), c_dqm[1].mark()])
        S2.close()

        S3 = ES.enter_context(contextlib.ExitStack())
        sb3 = lambda *a: S3.enter_context(nc.sbuf_tensor(*a))
        ps3 = lambda *a: S3.enter_context(nc.psum_tensor(*a))
        wproj_bf = sb3("wproj_bf", [128, 8, C], BF16)
        otc_ = [sb3(f"otc{i}", [128, 8, 128], BF16) for i in range(2)]
        xti_ = [sb3(f"xti{i}", [128, C], F32) for i in range(2)]
        x1t_ = [sb3(f"x1t{i}", [128, C], F32) for i in range(2)]
        xTf_ = [sb3(f"xTf{i}", [128, C], F32) for i in range(2)]
        ln1b_ = [sb3(f"ln1b{i}", [128, 8, 128], BF16) for i in range(2)]
        xn1_ = [sb3(f"xn1_{i}", [128, C], F32) for i in range(2)]
        gbuf = sb3("gbuf", [128, 10], F32)
        gb2 = sb3("gb2", [128, 10], F32)
        gwk = sb3("gwk", [128, 8], F32)
        gwk2 = sb3("gwk2", [128, 8], F32)
        pproj = [ps3(f"pproj{i}", [128, C], F32) for i in range(2)]
        ptr3 = ps3("ptr3", [128, C], F32)
        plog = ps3("plog", [128, 8], F32)
        pcwt = ps3("pcwt", [8, 128], F32)

        c_wpl = p.ctr("wpl")
        c_otl = [p.ctr("otl0"), p.ctr("otl1")]
        c_xtl = [p.ctr("xtl0"), p.ctr("xtl1")]
        c_sx1 = [p.ctr("sx10"), p.ctr("sx11")]
        c_sln = [p.ctr("sln0"), p.ctr("sln1")]
        c_scw = p.ctr("scw")
        c_p3mm = p.ctr("p3mm")
        c_p3v = p.ctr("p3v")
        c_p3a = p.ctr("p3a")
        c_p3tr = p.ctr("p3tr")

        wpv = wproj_in.rearrange("(cc p) f -> cc p f", p=128)
        for cc in range(8):
            p.op("gpsimd", lambda g, cc=cc: g.dma_start(out=wproj_bf[:, cc, :], in_=wpv[cc]),
                 inc=(c_wpl, 16))
        wp_done = c_wpl.mark()

        otcpool = Pool(otc_, c_otl)
        xtipool = Pool(xti_, c_xtl)
        x1pool = Pool(x1t_)
        xTfpool = Pool(xTf_)
        ln1pool = Pool(ln1b_)
        xn1pool = Pool(xn1_)
        projpool = Pool(pproj)
        x1store = [None, None]
        lnstore = [None, None]
        outv = out.rearrange("(n p) c -> n p c", p=128)
        x1v = x1_sc.rearrange("(n p) c -> n p c", p=128)
        ptr3_free = None
        plog_free = None
        pcwt_free = None

        for ts in range(NTS):
            otcl, ow, oi, occ = otcpool.acquire()
            for pc in range(8):
                p.op("sync", lambda sy, ts=ts, pc=pc, otcl=otcl: sy.dma_start(
                    out=otcl[:, pc, :], in_=ot_sc[pc][:, ts * 128:(ts + 1) * 128]),
                    waits=(ow if pc == 0 else []), inc=(occ, 16))
            otm = occ.mark()
            xtil, xw, xi, xcc = xtipool.acquire()
            p.op("sync", lambda sy, ts=ts, xtil=xtil: sy.dma_start(out=xtil[:], in_=xview[ts]),
                 waits=xw, inc=(xcc, 16))
            xtm = xcc.mark()
            ppl, ppw, ppi, _ = projpool.acquire()
            for half in range(2):
                for pc in range(8):
                    w = []
                    if pc == 0 and half == 0:
                        w = [otm, wp_done] + ppw
                    p.op("tensor", lambda t, pc=pc, half=half, ppl=ppl, otcl=otcl: t.matmul(
                        ppl[:, half * 512:(half + 1) * 512], otcl[:, pc, :],
                        wproj_bf[:, pc, half * 512:(half + 1) * 512],
                        start=(pc == 0), stop=False),
                        waits=w)
                p.op("tensor", lambda t, half=half, ppl=ppl: t.matmul(
                    ppl[:, half * 512:(half + 1) * 512], ones_b[0:1, :],
                    bprow[0:1, half * 512:(half + 1) * 512],
                    start=False, stop=True),
                    inc=(c_p3mm, 1))
            mmd = c_p3mm.mark()
            otcpool.release(oi, c_p3mm)
            x1l, x1w, x1i, _ = x1pool.acquire()
            p.op("vector", lambda v, x1l=x1l, ppl=ppl, xtil=xtil: v.scalar_tensor_tensor(
                x1l[:], ppl[:], 1.0, xtil[:], op0=ALU.mult, op1=ALU.add),
                waits=[mmd, xtm] + x1w, inc=(c_p3v, 1))
            xtipool.release(xi, c_p3v)
            projpool.release(ppi, c_p3v)
            x1_done = c_p3v.mark()
            p.op("scalar", lambda s, x1l=x1l: s.activation(junk[:], x1l[:], AF.Identity, bias=0.0, scale=1.0, accum_out=gbuf[:, 8:9]),
                 waits=[x1_done, c_p3v.mark()], inc=(c_p3a, 1))
            p.op("scalar", lambda s, x1l=x1l: s.activation(junk[:], x1l[:], AF.Square, bias=0.0, scale=1.0, accum_out=gbuf[:, 9:10]),
                 waits=[c_p3a.mark()], inc=(c_p3a, 1))
            p.op("vector", lambda v: v.tensor_scalar(gb2[:, 0:1], gbuf[:, 8:9], 1.0 / C, None, op0=ALU.mult),
                 waits=[c_p3a.mark()], inc=(c_p3v, 1))
            p.op("vector", lambda v: v.tensor_tensor(gb2[:, 1:2], gb2[:, 0:1], gb2[:, 0:1], op=ALU.mult),
                 waits=[c_p3v.mark()], inc=(c_p3v, 1))
            p.op("vector", lambda v: v.scalar_tensor_tensor(gb2[:, 1:2], gbuf[:, 9:10], 1.0 / C, gb2[:, 1:2], op0=ALU.mult, op1=ALU.subtract),
                 waits=[c_p3v.mark()], inc=(c_p3v, 1))
            p.op("scalar", lambda s: s.activation(gb2[:, 1:2], gb2[:, 1:2], AF.Ln, bias=epsc[:], scale=1.0),
                 waits=[c_p3v.mark()], inc=(c_p3a, 1))
            p.op("scalar", lambda s: s.activation(gb2[:, 2:3], gb2[:, 1:2], AF.Exp, bias=0.0, scale=-0.5),
                 waits=[c_p3a.mark()], inc=(c_p3a, 1))
            xn1l, xnw, xni, _ = xn1pool.acquire()
            p.op("vector", lambda v, x1l=x1l, xn1l=xn1l: v.tensor_scalar(
                xn1l[:], x1l[:], gb2[:, 0:1], gb2[:, 2:3], op0=ALU.subtract, op1=ALU.mult),
                waits=[c_p3a.mark()] + xnw, inc=(c_p3v, 1))
            xn1_done = c_p3v.mark()
            sxc = c_sx1[ts % 2]
            p.op("sync", lambda sy, ts=ts, x1l=x1l: sy.dma_start(out=x1v[ts], in_=x1l[:]),
                 waits=[xn1_done], inc=(sxc, 16))
            x1pool.release(x1i, sxc)
            for cc in range(8):
                w = [xn1_done]
                if cc == 0 and ptr3_free is not None:
                    w.append(ptr3_free)
                p.op("tensor", lambda t, cc=cc, xn1l=xn1l: t.transpose(
                    ptr3[:, cc * 128:(cc + 1) * 128], xn1l[:, cc * 128:(cc + 1) * 128], idf[:]),
                    waits=w, inc=(c_p3tr, 1))
            trd = c_p3tr.mark()
            xn1pool.release(xni, c_p3tr)
            xTfl, xfw, xfi, _ = xTfpool.acquire()
            p.op("vector", lambda v, xTfl=xTfl: v.tensor_copy(xTfl[:], ptr3[:]),
                 waits=[trd] + xfw, inc=(c_p3v, 1))
            ln1l, lnw, lni, _ = ln1pool.acquire()
            p.op("vector", lambda v, ln1l=ln1l: v.tensor_copy(
                ln1l[:], ptr3[:].rearrange("p (cc t) -> p cc t", t=128)),
                waits=[trd] + lnw, inc=(c_p3v, 1))
            ptr3_free = c_p3v.mark()
            slc = c_sln[ts % 2]
            p.op("scalar", lambda s, ts=ts, ln1l=ln1l: s.dma_start(
                out=lnx_sc[:, :, ts * 128:(ts + 1) * 128], in_=ln1l[:]),
                waits=[c_p3v.mark()], inc=(slc, 16))
            ln1pool.release(lni, slc)
            for cc in range(8):
                w = [c_p3v.mark()]
                if cc == 0 and plog_free is not None:
                    w.append(plog_free)
                p.op("tensor", lambda t, cc=cc, xTfl=xTfl: t.matmul(
                    plog[:], xTfl[:, cc * 128:(cc + 1) * 128], wgT[:, cc, :],
                    start=(cc == 0), stop=(cc == 7)),
                    waits=w, inc=(c_p3mm, 1) if cc == 7 else None)
            xTfpool.release(xfi, c_p3mm)
            gmm = c_p3mm.mark()
            p.op("vector", lambda v: v.reduce_max(gbuf[:, 8:9], plog[:], axis=AX.X, negate=True),
                 waits=[gmm, c_p3v.mark()], inc=(c_p3v, 1))
            p.op("scalar", lambda s: s.activation(gbuf[:, 0:8], plog[:], AF.Exp, bias=gbuf[:, 8:9], scale=1.0, accum_out=gbuf[:, 9:10]),
                 waits=[c_p3v.mark()], inc=(c_p3a, 1))
            plog_free = c_p3a.mark()
            p.op("vector", lambda v: v.reciprocal(gb2[:, 3:4], gbuf[:, 9:10]),
                 waits=[c_p3a.mark()], inc=(c_p3v, 1))
            p.op("vector", lambda v: v.reduce_max(gb2[:, 4:5], gbuf[:, 0:8], axis=AX.X),
                 waits=[c_p3a.mark()], inc=(c_p3v, 1))
            p.op("vector", lambda v: v.tensor_scalar(gwk[:], gbuf[:, 0:8], gb2[:, 4:5], None, op0=ALU.is_ge),
                 waits=[c_p3v.mark()], inc=(c_p3v, 1))
            p.op("vector", lambda v: v.scalar_tensor_tensor(gwk2[:], gwk[:], -1e30, gbuf[:, 0:8], op0=ALU.mult, op1=ALU.add),
                 waits=[c_p3v.mark()], inc=(c_p3v, 1))
            p.op("vector", lambda v: v.reduce_max(gb2[:, 5:6], gwk2[:], axis=AX.X),
                 waits=[c_p3v.mark()], inc=(c_p3v, 1))
            p.op("vector", lambda v: v.tensor_scalar(gwk[:], gbuf[:, 0:8], gb2[:, 5:6], None, op0=ALU.is_ge),
                 waits=[c_p3v.mark()], inc=(c_p3v, 1))
            p.op("vector", lambda v: v.tensor_tensor(gwk2[:], gwk[:], gbuf[:, 0:8], op=ALU.mult),
                 waits=[c_p3v.mark()], inc=(c_p3v, 1))
            p.op("vector", lambda v, ts=ts: v.tensor_scalar(cw_all[:, ts, :], gwk2[:], gb2[:, 3:4], None, op0=ALU.mult),
                 waits=[c_p3v.mark()], inc=(c_p3v, 1))
            cw_done = c_p3v.mark()
            if debug:
                p.op("scalar", lambda s, ts=ts: s.dma_start(out=cw_sc[:, ts, :], in_=cw_all[:, ts, :]),
                     waits=[cw_done], inc=(c_scw, 16))
            w = [cw_done]
            if pcwt_free is not None:
                w.append(pcwt_free)
            p.op("tensor", lambda t, ts=ts: t.transpose(pcwt[:], cw_all[:, ts, :], idf[:]),
                 waits=w, inc=(c_p3tr, 1))
            p.op("vector", lambda v, ts=ts: v.tensor_copy(cwT[:, ts * 128:(ts + 1) * 128], pcwt[:]),
                 waits=[c_p3tr.mark()], inc=(c_p3v, 1))
            pcwt_free = c_p3v.mark()

        p.barrier("end_p3", extra_waits=[c_sx1[0].mark(), c_sx1[1].mark(),
                                         c_sln[0].mark(), c_sln[1].mark(), c_scw.mark(),
                                         c_otl[0].mark(), c_otl[1].mark(),
                                         c_xtl[0].mark(), c_xtl[1].mark(), c_wpl.mark()])
        S3.close()

        S5 = ES.enter_context(contextlib.ExitStack())
        sb5 = lambda *a: S5.enter_context(nc.sbuf_tensor(*a))
        ps5 = lambda *a: S5.enter_context(nc.psum_tensor(*a))
        lnxg = sb5("lnxg", [128, 8, GT], BF16)
        x1g_ = [sb5(f"x1g{i}", [128, C], F32) for i in range(2)]
        w1s_ = [sb5(f"w1s{i}", [128, 8, 512], BF16) for i in range(2)]
        w2h_ = [sb5(f"w2h{i}", [128, 16, C], BF16) for i in range(2)]
        ht_ = [sb5(f"ht{i}", [128, 16, GT], BF16) for i in range(1)]
        y_acc = sb5("y_acc", [128, 8, C], F32)
        php = [ps5(f"php{i}", [128, GT], F32) for i in range(2)]
        pyp = [ps5(f"pyp{i}", [128, C], F32) for i in range(2)]

        c_w1 = [p.ctr(f"w1_{i}") for i in range(2)]
        c_w2 = [p.ctr(f"w2_{i}") for i in range(2)]
        c_lnl = p.ctr("lnx_ld")
        c_x1g = [p.ctr("x1g0"), p.ctr("x1g1")]
        c_ffm = p.ctr("ffm")
        c_gel = p.ctr("gel")
        c_yev = p.ctr("yev")
        c_outs = [p.ctr(f"outs{i}") for i in range(8)]

        w1pool = Pool(w1s_, c_w1)
        w2pool = Pool(w2h_, c_w2)
        htpool = Pool(ht_)
        hpool = Pool(php)
        ypool = Pool(pyp)
        x1gpool = Pool(x1g_, c_x1g)
        ylast = [None] * 8
        outlast = [None] * 8
        lnxg_free = None

        for g in range(NG):
            w = [lnxg_free] if lnxg_free is not None else []
            p.op("sync", lambda sy, g=g: sy.dma_start(
                out=lnxg[:], in_=lnx_sc[:, :, g * GT:(g + 1) * GT]),
                waits=w, inc=(c_lnl, 16))
            lnx_ld = c_lnl.mark()
            for ts in range(8):
                x1l, xw, xi, xcc = x1gpool.acquire()
                p.op("sync", lambda sy, g=g, ts=ts, x1l=x1l: sy.dma_start(out=x1l[:], in_=x1v[g * 8 + ts]),
                     waits=xw, inc=(xcc, 16))
                x1m = xcc.mark()
                yp, yw, yi, _ = ypool.acquire()
                tok = (g * 8 + ts) * 128
                for half in range(2):
                    p.op("tensor", lambda t, half=half, yp=yp, tok=tok: t.matmul(
                        yp[:, half * 512:(half + 1) * 512], cwT[:, tok:tok + 128],
                        b2row[:, half * 512:(half + 1) * 512], start=True, stop=True),
                        waits=(yw if half == 0 else []), inc=(c_ffm, 1) if half == 1 else None)
                w = [c_ffm.mark(), x1m]
                if outlast[ts] is not None:
                    w.append(outlast[ts])
                if ylast[ts] is not None:
                    w.append(ylast[ts])
                p.op("vector", lambda v, ts=ts, yp=yp, x1l=x1l: v.tensor_tensor(
                    y_acc[:, ts, :], yp[:], x1l[:], op=ALU.add),
                    waits=w, inc=(c_yev, 1))
                ylast[ts] = c_yev.mark()
                ypool.release(yi, c_yev)
                x1gpool.release(xi, c_yev)
            w1_jobs = [(e_, h_ * 4 + si_) for e_ in range(E) for h_ in range(2) for si_ in range(4)]
            w1_fifo = []
            w1_cur = [0]

            def prefetch_w1():
                if w1_cur[0] >= len(w1_jobs):
                    return
                e_, s_ = w1_jobs[w1_cur[0]]
                w1_cur[0] += 1
                w1l, w1w, w1i, w1c = w1pool.acquire()
                w1view = w1_in[e_].rearrange("(cg p) (s f) -> s p cg f", p=128, f=512)
                p.op("gpsimd", lambda gq, w1l=w1l, w1view=w1view, s_=s_: gq.dma_start(
                    out=w1l[:], in_=w1view[s_]),
                    waits=w1w, inc=(w1c, 16))
                w1_fifo.append((w1l, w1c.mark(), w1i))

            prefetch_w1()
            prefetch_w1()
            for e in range(E):
                for half in range(2):
                    htl, htw, hti, _ = htpool.acquire()
                    w2l, w2w, w2i, w2c = w2pool.acquire()
                    w2view = w2_in[e].rearrange("(hh hc p) d -> hh p hc d", hc=16, p=128)
                    p.op("gpsimd", lambda gq, w2l=w2l, w2view=w2view, half=half: gq.dma_start(
                        out=w2l[:], in_=w2view[half]),
                        waits=w2w, inc=(w2c, 16))
                    w2m = w2c.mark()
                    for si in range(4):
                        s = half * 4 + si
                        w1l, w1m, w1i = w1_fifo.pop(0)
                        for j in range(4):
                            hf = s * 4 + j
                            hfh = si * 4 + j
                            hp, hw, hi, _ = hpool.acquire()
                            for cbk in range(8):
                                for th in range(2):
                                    w = []
                                    if cbk == 0 and th == 0:
                                        w = hw + [w1m, lnx_ld]
                                    p.op("tensor", lambda t, hp=hp, w1l=w1l, cbk=cbk, j=j, th=th: t.matmul(
                                        hp[:, th * 512:(th + 1) * 512],
                                        w1l[:, cbk, j * 128:(j + 1) * 128],
                                        lnxg[:, cbk, th * 512:(th + 1) * 512],
                                        start=(cbk == 0), stop=(cbk == 7)),
                                        waits=w,
                                        inc=(c_ffm, 1) if (cbk == 7 and th == 1) else None)
                            w = [c_ffm.mark()]
                            if hfh == 0:
                                w += htw
                            p.op("scalar", lambda sc, htl=htl, hfh=hfh, hp=hp, hf=hf, e=e: sc.activation(
                                htl[:, hfh, :], hp[:], AF.Gelu, bias=b1T[:, hf, e:e + 1], scale=1.0),
                                waits=w, inc=(c_gel, 1))
                            hpool.release(hi, c_gel)
                        w1pool.release(w1i, c_ffm)
                        prefetch_w1()
                    a_done = c_gel.mark()
                    if debug and g == 0 and e == 0 and half == 0:
                        p.op("sync", lambda sy, htl=htl: sy.dma_start(out=ht_sc[:], in_=htl[:]),
                             waits=[a_done], inc=(c_lnl, 16))
                    for ts in range(8):
                        yp, yw, yi, _ = ypool.acquire()
                        for hfh in range(16):
                            for h2 in range(2):
                                w = []
                                if hfh == 0 and h2 == 0:
                                    w = yw + [a_done, w2m]
                                p.op("tensor", lambda t, yp=yp, htl=htl, w2l=w2l, hfh=hfh, h2=h2, ts=ts: t.matmul(
                                    yp[:, h2 * 512:(h2 + 1) * 512],
                                    htl[:, hfh, ts * 128:(ts + 1) * 128],
                                    w2l[:, hfh, h2 * 512:(h2 + 1) * 512],
                                    start=(hfh == 0), stop=(hfh == 15)),
                                    waits=w,
                                    inc=(c_ffm, 1) if (hfh == 15 and h2 == 1) else None)
                        mmd = c_ffm.mark()
                        p.op("vector", lambda v, yp=yp, g=g, ts=ts, e=e: v.scalar_tensor_tensor(
                            y_acc[:, ts, :], yp[:], cw_all[:, g * 8 + ts, e:e + 1],
                            y_acc[:, ts, :], op0=ALU.mult, op1=ALU.add),
                            waits=[mmd, ylast[ts]], inc=(c_yev, 1))
                        ylast[ts] = c_yev.mark()
                        ypool.release(yi, c_yev)
                        if debug and g == 0 and e == 0 and half == 1:
                            p.op("sync", lambda sy, ts=ts: sy.dma_start(
                                out=y0_sc[:, ts, :], in_=y_acc[:, ts, :]),
                                waits=[ylast[ts]], inc=(c_outs[ts], 16))
                            ylast[ts] = (c_yev, c_yev.val)
                            outlast[ts] = c_outs[ts].mark() if outlast[ts] is None else outlast[ts]
                            ylast_dump = c_outs[ts].mark()
                            ylast[ts] = ylast_dump
                        if e == E - 1 and half == 1:
                            p.op("sync", lambda sy, g=g, ts=ts: sy.dma_start(
                                out=outv[g * 8 + ts], in_=y_acc[:, ts, :]),
                                waits=[ylast[ts]], inc=(c_outs[ts], 16))
                            outlast[ts] = c_outs[ts].mark()
                    htpool.release(hti, c_ffm)
                    w2pool.release(w2i, c_ffm)
            lnxg_free = c_ffm.mark()

        for ts in range(8):
            p.wait("sync", (c_outs[ts], c_outs[ts].val))
        with nc.Block() as block:
            p.build(block)
    return nc


_IDF = np.eye(128, dtype=np.float32)
_IDB = np.eye(128).astype(ml_dtypes.bfloat16)


def kernel(L0_attn_x, L0_attn_sin, L0_attn_cos, L0_attn_w_qkv, L0_attn_b_qkv,
           L0_attn_w_proj, L0_attn_b_proj, L0_ffn_w_gate, L0_ffn_W1, L0_ffn_b1,
           L0_ffn_W2, L0_ffn_b2):
    x = np.ascontiguousarray(np.asarray(L0_attn_x, dtype=np.float32))
    common = {
        "sin": np.ascontiguousarray(np.asarray(L0_attn_sin, dtype=np.float32)),
        "cos": np.ascontiguousarray(np.asarray(L0_attn_cos, dtype=np.float32)),
        "wqkv": np.ascontiguousarray(np.asarray(L0_attn_w_qkv, dtype=np.float32)),
        "bqkv": np.ascontiguousarray(np.asarray(L0_attn_b_qkv, dtype=np.float32)),
        "wproj": np.ascontiguousarray(np.asarray(L0_attn_w_proj, dtype=np.float32)),
        "bproj": np.ascontiguousarray(np.asarray(L0_attn_b_proj, dtype=np.float32)),
        "wg": np.ascontiguousarray(np.asarray(L0_ffn_w_gate, dtype=np.float32)),
        "w1": np.ascontiguousarray(np.asarray(L0_ffn_W1, dtype=np.float32)),
        "b1": np.ascontiguousarray(np.asarray(L0_ffn_b1, dtype=np.float32)),
        "w2": np.ascontiguousarray(np.asarray(L0_ffn_W2, dtype=np.float32)),
        "b2": np.ascontiguousarray(np.asarray(L0_ffn_b2, dtype=np.float32)),
        "idf": _IDF,
        "idb": _IDB,
    }
    nc = build_nc(debug=False)
    in_maps = [dict(common, x=np.ascontiguousarray(x[b])) for b in range(8)]
    res = run_bass_kernel_spmd(nc, in_maps, core_ids=list(range(8)))
    return np.stack([np.asarray(res.results[b]["out"]) for b in range(8)], axis=0)
